# version 05 (final)
# speedup vs baseline: 1.3315x; 1.0434x over previous
import sys
import os
import numpy as np
import contextlib

sys.path.insert(0, '/opt/trn_rl_repo')
import concourse.bass as bass
import concourse.mybir as mybir
from concourse.bass_utils import run_bass_kernel_spmd
import ml_dtypes

F32 = mybir.dt.float32
BF16 = mybir.dt.bfloat16
AF = mybir.ActivationFunctionType
from concourse.alu_op_type import AluOpType as ALU
AX = mybir.AxisListType

NCORES = 8
T = 2048
D = 1024
DC = D // 128
TB = 512
NTB = T // TB
BF = ml_dtypes.bfloat16


def _t(a):
    return np.ascontiguousarray(np.swapaxes(np.asarray(a), -1, -2))


def _row(b):
    return np.ascontiguousarray(np.asarray(b, dtype=np.float32).reshape(1, -1)).astype(BF)



class Ctx:
    def __init__(self, nc, es):
        self.nc = nc
        self.es = es
        self.sem = {}
        self.val = {}
        self.dram = {}
        self.sb = {}
        self.uid = 0
        self.bank_ev = [0] * 6
        self.tps_ev = [0, 0]
        self.rps_ev = [0, 0]
        self.t2ps_ev = [0, 0]
        self.ops_ev = [0, 0]

    def new_sem(self, name):
        s = self.es.enter_context(self.nc.semaphore(name))
        self.sem[name] = s
        self.val[name] = 0
        return s

    def inc(self, inst, name, n=1):
        inst.then_inc(self.sem[name], n)
        self.val[name] += n
        return self.val[name]

    def wait(self, eng, name, val=None):
        eng.wait_ge(self.sem[name], self.val[name] if val is None else val)

    def barrier(self):
        self.nc.all_engine_barrier()

    def sbuf(self, name, shape, dtype, scope=None):
        self.uid += 1
        h = (scope or self.es).enter_context(
            self.nc.sbuf_tensor("s%d_%s" % (self.uid, name), list(shape), dtype))
        self.sb[name] = h
        return h

    def dscratch(self, name, shape, dtype):
        self.uid += 1
        return self.nc.dram_tensor("d%d_%s" % (self.uid, name), list(shape), dtype).ap()

    def psum(self, name, shape, dtype=F32, scope=None):
        self.uid += 1
        h = (scope or self.es).enter_context(
            self.nc.psum_tensor("p%d_%s" % (self.uid, name), list(shape), dtype))
        self.sb[name] = h
        if name == "proj_ps":
            self.bank_ev = [self.val["ev"]] * shape[1]
        return h



def emit_ln(ctx, X, out_bf, t0, t1, out_f32=None, f32_t0=0, gamma=None, beta=None):
    if gamma is not None:
        ctx.wait(ctx.nc.vector, "ldw")
    ctx.wait(ctx.nc.scalar, "ev")
    """out_bf[:, c, t] = LN(X)[:, c, t] for t in [t0, t1); optional f32 copy
    written at out_f32[:, c, f32_t0 + (t - t0)]."""
    nc = ctx.nc
    V, SC, PE = nc.vector, nc.scalar, nc.tensor
    ones = ctx.sb["c_ones128"]
    scr = ctx.sb["scr"]
    sq = scr[:, 0:TB]
    tmp = scr[:, TB:2 * TB]
    brstd = scr[:, 2 * TB:3 * TB]
    ps = ctx.sb["ln_ps"]
    for o in range(t0, t1, TB):
        for ci in range(DC):
            ctx.wait(SC, "mm")
            ctx.inc(SC.activation(sq, X[:, ci, o:o + TB], AF.Square), "sc")
            ctx.wait(PE, "sc")
            ctx.inc(PE.matmul(ps[:, 1, :], ones[:, :], sq,
                              start=(ci == 0), stop=(ci == DC - 1)), "mm")
            ctx.inc(PE.matmul(ps[:, 0, :], ones[:, :], X[:, ci, o:o + TB],
                              start=(ci == 0), stop=(ci == DC - 1)), "mm")
        ctx.wait(SC, "mm")
        ctx.inc(SC.activation(brstd, ps[:, 0, :], AF.Square), "sc")
        ctx.wait(V, "sc")
        ctx.inc(V.tensor_tensor(brstd, ps[:, 1, :], brstd, ALU.subtract), "ev")
        ctx.wait(SC, "ev")
        ctx.inc(SC.activation(brstd, brstd, AF.Ln, bias=np.float32(1e-5)), "sc")
        ctx.wait(SC, "sc")
        ctx.inc(SC.activation(brstd, brstd, AF.Exp, scale=np.float32(-0.5)), "sc")
        ctx.wait(V, "sc")
        for ci in range(DC):
            ctx.wait(V, "ev")
            ctx.inc(V.tensor_tensor(tmp, X[:, ci, o:o + TB], ps[:, 0, :], ALU.subtract), "ev")
            ctx.wait(V, "ev")
            if gamma is not None:
                ctx.inc(V.tensor_tensor(tmp, tmp, brstd, ALU.mult), "ev")
                ctx.wait(V, "ev")
                ctx.inc(V.tensor_scalar(out_bf[:, ci, o:o + TB], tmp,
                                        gamma[:, ci:ci + 1], beta[:, ci:ci + 1],
                                        ALU.mult, ALU.add), "ev")
            else:
                ctx.inc(V.tensor_tensor(out_bf[:, ci, o:o + TB], tmp, brstd, ALU.mult), "ev")
            if out_f32 is not None:
                ctx.wait(V, "ev")
                ctx.inc(V.tensor_tensor(out_f32[:, ci, f32_t0 + o - t0: f32_t0 + o - t0 + TB],
                                        tmp, brstd, ALU.mult), "ev")
        ctx.wait(PE, "ev")



def load_weight_resident(ctx, wname, sbname, KC, Mdim, scope=None):
    h = ctx.sbuf(sbname, [128, KC, Mdim], BF16, scope=scope)
    ctx.inc(ctx.nc.gpsimd.dma_start(
        h[:, :, :], ctx.dram[wname].rearrange("(c p) m -> p c m", p=128),
        allow_slow_non_contiguous=(Mdim < 8)), "ldw", 16)
    return h


def load_row(ctx, name, scope=None, dt=None):
    arrshape = ctx.dram[name].shape
    if dt is None:
        dt = ctx.dram[name].dtype
    h = ctx.sbuf(name, list(arrshape), dt, scope=scope)
    ctx.inc(ctx.nc.gpsimd.dma_start(h[...], ctx.dram[name]), "ldw", 16)
    return h


def emit_proj(ctx, w, src, KC, tlen, m_sizes,
              bias_row=None, act=None, act_scale=None,
              dst=None, resid=None, resid_scale=1.0,
              tile_n=512, src_t0=0, dst_t0=0, evac=None,
              ps_name="proj_ps", src_list=None, wait_src=(), out_scale=None):
    nc = ctx.nc
    V, SC, PE, G = nc.vector, nc.scalar, nc.tensor, nc.gpsimd
    MC = len(m_sizes)
    NT = (tlen + tile_n - 1) // tile_n
    ps = ctx.sb[ps_name]
    ones1 = ctx.sb["c_ones1"]
    streamed = isinstance(w, str)
    moff = [0]
    for s in m_sizes[:-1]:
        moff.append(moff[-1] + s)

    if streamed:
        wap = ctx.dram[w]
        wbuf = ctx.sb["wbuf"]

        def issue_load(m):
            msz2 = m_sizes[m]
            ctx.wait(G, "mm")
            ctx.inc(G.dma_start(wbuf[:, m % 2, 0:KC, 0:msz2],
                                wap[:, moff[m]:moff[m] + msz2].rearrange("(c p) m -> p c m", p=128),
                                allow_slow_non_contiguous=(msz2 < 8)),
                    "ldwA" if m % 2 == 0 else "ldwB", 16)

        issue_load(0)
        if MC > 1:
            issue_load(1)

    ctx.wait(PE, "ldw")
    for s in wait_src:
        ctx.wait(PE, s)
    for m in range(MC):
        msz = m_sizes[m]
        if streamed:
            ctx.wait(PE, "ldwA" if m % 2 == 0 else "ldwB")
            wlist = [lambda c, msz=msz, m=m: ctx.sb["wbuf"][:, m % 2, c, 0:msz]]
        elif isinstance(w, list):
            wlist = [(lambda c, msz=msz, m=m, wi=wi: wi[:, c, moff[m]:moff[m] + msz]) for wi in w]
        else:
            wlist = [lambda c, msz=msz, m=m: w[:, c, moff[m]:moff[m] + msz]]
        for n in range(NT):
            nsz = min(tile_n, tlen - n * tile_n)
            bank = ctx.val["mm"] % ps.shape[1]
            PE.wait_ge(ctx.sem["ev"], ctx.bank_ev[bank])
            for wj, wtile in enumerate(wlist):
                for c in range(KC):
                    last = (wj == len(wlist) - 1) and (c == KC - 1) and bias_row is None
                    inst = PE.matmul(ps[0:msz, bank, 0:nsz], wtile(c),
                                     src[:, c, src_t0 + n * tile_n: src_t0 + n * tile_n + nsz],
                                     start=(wj == 0 and c == 0), stop=last)
            if bias_row is not None:
                br = ctx.sb[bias_row] if isinstance(bias_row, str) else bias_row
                inst = PE.matmul(ps[0:msz, bank, 0:nsz],
                                 br[0:1, moff[m]:moff[m] + msz],
                                 ones1[0:1, 0:nsz], start=False, stop=True)
            ctx.inc(inst, "mm")
            t0o = dst_t0 + n * tile_n
            psl = ps[0:msz, bank, 0:nsz]
            if evac is not None:
                evac(m, n, nsz, bank, moff[m])
            elif act is not None:
                SC.wait_ge(ctx.sem["mm"], ctx.val["mm"])
                ctx.inc(SC.activation(dst[0:msz, m, t0o:t0o + nsz], psl, act, scale=np.float32(1.0 if act_scale is None else act_scale)), "ev")
            elif resid is not None:
                V.wait_ge(ctx.sem["mm"], ctx.val["mm"])
                rsl = resid[0:msz, m, t0o:t0o + nsz]
                ctx.inc(V.scalar_tensor_tensor(rsl, psl, resid_scale, rsl, ALU.mult, ALU.add), "ev")
            else:
                V.wait_ge(ctx.sem["mm"], ctx.val["mm"])
                if out_scale is None:
                    ctx.inc(V.tensor_copy(dst[0:msz, m, t0o:t0o + nsz], psl), "ev")
                else:
                    ctx.inc(V.tensor_scalar(dst[0:msz, m, t0o:t0o + nsz], psl,
                                            float(out_scale), None, ALU.mult), "ev")
            ctx.bank_ev[bank] = ctx.val["ev"]
        if streamed and m + 2 < MC:
            issue_load(m + 2)


def emit_proj_tok(ctx, w, src, KC, Mdim, dst, bias_row=None, ntc=16,
                  out_scale=None, wait_src=(), dst_col0=0):
    nc = ctx.nc
    V, SC, PE, G = nc.vector, nc.scalar, nc.tensor, nc.gpsimd
    ps = ctx.sb["proj_ps"]
    nb = ps.shape[1]
    ones1 = ctx.sb["c_ones1"]
    wap = ctx.dram[w]
    wbuf = ctx.sb["wbuf"]
    NJ = (Mdim + 511) // 512

    def issue_load_tok(j):
        csz2 = min(512, Mdim - j * 512)
        ctx.wait(G, "mm")
        ctx.inc(G.dma_start(wbuf[:, j % 2, 0:KC, 0:csz2],
                            wap[:, j * 512:j * 512 + csz2].rearrange("(c p) m -> p c m", p=128),
                            allow_slow_non_contiguous=(csz2 < 8)),
                "ldwA" if j % 2 == 0 else "ldwB", 16)

    issue_load_tok(0)
    if NJ > 1:
        issue_load_tok(1)
    ctx.wait(PE, "ldw")
    for st in wait_src:
        ctx.wait(PE, st)
    for j in range(NJ):
        csz = min(512, Mdim - j * 512)
        ctx.wait(PE, "ldwA" if j % 2 == 0 else "ldwB")
        for tc in range(ntc):
            bank = ctx.val["mm"] % nb
            PE.wait_ge(ctx.sem["ev"], ctx.bank_ev[bank])
            for c in range(KC):
                inst = PE.matmul(ps[:, bank, 0:csz], src[:, c, 128 * tc:128 * (tc + 1)],
                                 wbuf[:, j % 2, c, 0:csz], start=(c == 0),
                                 stop=(c == KC - 1 and bias_row is None))
            if bias_row is not None:
                br = ctx.sb[bias_row]
                inst = PE.matmul(ps[:, bank, 0:csz], ones1[0:1, 0:128],
                                 br[0:1, j * 512:j * 512 + csz], start=False, stop=True)
            ctx.inc(inst, "mm")
            V.wait_ge(ctx.sem["mm"], ctx.val["mm"])
            dsl = dst[:, tc, dst_col0 + j * 512: dst_col0 + j * 512 + csz]
            if out_scale is None:
                ctx.inc(V.tensor_copy(dsl, ps[:, bank, 0:csz]), "ev")
            else:
                ctx.inc(V.tensor_scalar(dsl, ps[:, bank, 0:csz], float(out_scale),
                                        None, ALU.mult), "ev")
            ctx.bank_ev[bank] = ctx.val["ev"]
        if j + 2 < NJ:
            issue_load_tok(j + 2)



def emit_l0_attn(ctx):
    nc = ctx.nc
    V, SC, PE, G = nc.vector, nc.scalar, nc.tensor, nc.gpsimd
    X, XH = ctx.sb["X"], ctx.sb["XH"]
    NH, HD, CH = 16, 64, 128
    NCH = T // CH
    es = contextlib.ExitStack()
    with es:
        xpool = X[:, :, :].rearrange("p c t -> p (c t)").bitcast(BF16)
        QF = xpool[:, 0:DC * T].rearrange("p (c t) -> p c t", t=T)
        KF = xpool[:, DC * T:2 * DC * T].rearrange("p (c t) -> p c t", t=T)
        xsp = ctx.nc.dram_tensor("l0a_xspill", [D, T], F32).ap()
        VT = ctx.sbuf("l0a_vt", [128, NCH, D], BF16, es)
        for rn in ["L0a_bqr", "L0a_bkr", "L0a_bvr", "L0a_bor", "c_avgc", "c_cmask"]:
            load_row(ctx, rn, es)
        AO = XH
        with contextlib.ExitStack() as esA:
            ctx.psum("ln_ps", [128, 2, TB], F32, scope=esA)
            ctx.psum("proj_ps", [128, 6, 512], F32, scope=esA)
            ctx.sbuf("wbuf", [128, 2, 8, 512], BF16, scope=esA)
            emit_ln(ctx, X, XH, 0, T)
            ctx.inc(nc.sync.dma_start(xsp.rearrange("(c p) t -> p c t", p=128), X[:, :, :]), "st", 16)
            nc.sync.wait_ge(ctx.sem["st"], ctx.val["st"])
            ctx.barrier()
            emit_proj(ctx, "L0a_wq", XH, 8, T, [128] * 8, bias_row="L0a_bqr",
                      dst=QF, out_scale=0.125, wait_src=("ev",))
            emit_proj(ctx, "L0a_wk", XH, 8, T, [128] * 8, bias_row="L0a_bkr", dst=KF)
            emit_proj_tok(ctx, "L0a_wv", XH, 8, D, VT, bias_row="L0a_bvr")
        if os.environ.get("DBG_L0A"):
            dq = nc.dram_tensor("dbg_qf", [D, T], BF16, kind="ExternalOutput").ap()
            dk = nc.dram_tensor("dbg_kf", [D, T], BF16, kind="ExternalOutput").ap()
            dv = nc.dram_tensor("dbg_vt", [T, D], BF16, kind="ExternalOutput").ap()
            nc.sync.wait_ge(ctx.sem["ev"], ctx.val["ev"])
            ctx.inc(nc.sync.dma_start(dq.rearrange("(c p) t -> p c t", p=128), QF), "st", 16)
            ctx.inc(nc.sync.dma_start(dk.rearrange("(c p) t -> p c t", p=128), KF), "st", 16)
            ctx.inc(nc.sync.dma_start(dv.rearrange("(tc p) f -> p tc f", p=128), VT[:, :, :]), "st", 16)
            nc.sync.wait_ge(ctx.sem["st"], ctx.val["st"])
        KM = ctx.sbuf("l0a_km", [128, DC, NCH], BF16, es)
        VM = ctx.sbuf("l0a_vm", [16, D], BF16, es)
        scr = ctx.sb["scr"]
        ctx.wait(V, "ev")
        for c in range(DC):
            ctx.wait(V, "ev")
            ctx.inc(V.tensor_reduce(scr[:, 0:NCH], KF[:, c, :].rearrange("p (j w) -> p j w", w=CH),
                                    axis=AX.X, op=ALU.add), "ev")
            ctx.wait(V, "ev")
            ctx.inc(V.tensor_scalar(KM[:, c, :], scr[:, 0:NCH], 1.0 / CH, None, ALU.mult), "ev")
        with contextlib.ExitStack() as esB:
            vm_ps = ctx.psum("l0a_vm_ps", [16, 2, 512], F32, scope=esB)
            avgc = ctx.sb["c_avgc"]
            ctx.wait(PE, "ev")
            for half in range(2):
                for tc in range(NCH):
                    ctx.inc(PE.matmul(vm_ps[0:16, half, :], avgc[:, 16 * tc:16 * (tc + 1)],
                                      VT[:, tc, 512 * half:512 * (half + 1)],
                                      start=(tc == 0), stop=(tc == NCH - 1)), "mm")
            ctx.wait(V, "mm")
            for half in range(2):
                ctx.inc(V.tensor_copy(VM[0:16, 512 * half:512 * (half + 1)], vm_ps[0:16, half, :]), "ev")
            ctx.wait(PE, "ev")

        if os.environ.get("L0A_TRUNC") == "A":
            return
        with contextlib.ExitStack() as esB2:
            s_ps = ctx.psum("l0a_s_ps", [128, 8, CH], F32, scope=esB2)
            s2_ps_r = ctx.psum("l0a_s2_ps", [128, 512], F32, scope=esB2)
            s2_ps = s2_ps_r[:, 0:256].rearrange("p (h j) -> p h j", j=16)
            o_ps_r = ctx.psum("l0a_o_ps", [64, 2, 512], F32, scope=esB2)
            o_ps = o_ps_r[:, :, 0:CH]
            t_ps_r = ctx.psum("l0a_t_ps", [128, 2, 1024], BF16, scope=esB2)
            t_ps = t_ps_r[:, :, 0:CH]
            t2_ps = t_ps_r[0:16, :, 512:512 + CH]
            EB = ctx.sbuf("l0a_e", [128, 8, CH], BF16, es)
            E2B = ctx.sbuf("l0a_e2", [128, NCH, 16], BF16, es)
            ETB = ctx.sbuf("l0a_et", [128, 8, CH], BF16, es)
            E2TB = ctx.sbuf("l0a_e2t", [16, NCH, CH], BF16, es)
            NRM = ctx.sbuf("l0a_nrm", [128, NCH], F32, es)
            DD = ctx.sbuf("l0a_dd", [128, NCH], F32, es)
            ident = ctx.sb["c_ident"]
            cmask = ctx.sb["c_cmask"]
            QO = ctx.sbuf("l0a_qo", [64, DC, CH], BF16, es)
            KO = ctx.sbuf("l0a_ko", [64, DC, CH], BF16, es)
            KMO = ctx.sbuf("l0a_kmo", [64, DC, NCH], BF16, es)
            ctx.wait(V, "ev")
            ctx.inc(V.tensor_copy(KMO[:, :, :], KM[64:128, :, :]), "ev")

            def hrows(h):
                return slice((h % 2) * 64, (h % 2) * 64 + 64)

            def qop(h, csl):
                return QF[0:64, h // 2, csl] if h % 2 == 0 else QO[:, h // 2, :]

            def kop(h, csl):
                return KF[0:64, h // 2, csl] if h % 2 == 0 else KO[:, h // 2, :]

            def kmop(h):
                return KM[0:64, h // 2, :] if h % 2 == 0 else KMO[:, h // 2, :]

            for c in range(int(os.environ.get("L0A_NCH", NCH))):
                csl = slice(CH * c, CH * (c + 1))
                ctx.wait(V, "mm")
                ctx.inc(V.tensor_copy(QO[:, :, :], QF[64:128, :, csl]), "ev")
                ctx.wait(V, "ev")
                ctx.inc(V.tensor_copy(KO[:, :, :], KF[64:128, :, csl]), "ev")
                ctx.wait(PE, "ev")
                for h in range(NH):
                    ctx.inc(PE.matmul(s2_ps[:, h, :], qop(h, csl),
                                      kmop(h), start=True, stop=True), "mm")
                ctx.wait(V, "mm")
                ctx.wait(V, "sc")
                sc2 = scr[:, 1024:1024 + NCH * 16].rearrange("p (h j) -> p h j", j=16)
                cm = cmask[:, 16 * c:16 * (c + 1)].unsqueeze(1).broadcast_to([128, 16, 16])
                ctx.inc(V.tensor_tensor(sc2, s2_ps[:, :, :], cm, ALU.add), "ev")
                ctx.wait(V, "ev")
                nrm2 = scr[:, 1920:1936]
                ctx.inc(V.tensor_reduce(nrm2, sc2, axis=AX.X, op=ALU.max, negate=True), "ev")
                ctx.wait(V, "ev")
                nrm2e = nrm2.broadcast_to([128, 16, 16])
                ctx.inc(V.tensor_tensor(sc2, sc2, nrm2e, ALU.add), "ev")
                ctx.wait(SC, "ev")
                ctx.inc(SC.activation(E2B[:, :, :], sc2, AF.Exp), "sc")
                S2 = scr[:, 1936:1952]
                ctx.wait(V, "sc")
                ctx.inc(V.tensor_reduce(S2, E2B[:, :, :], axis=AX.X, op=ALU.add), "ev")
                ctx.wait(V, "ev")
                ctx.inc(V.reciprocal(S2, S2), "ev")
                if os.environ.get("L0A_SUB") == "s2":
                    continue
                for hh in range(2):
                    ctx.wait(PE, "ev")
                    for i in range(8):
                        h = hh * 8 + i
                        ctx.inc(PE.matmul(s_ps[:, i, :], qop(h, csl),
                                          kop(h, csl), start=True, stop=True), "mm")
                    if os.environ.get("L0A_SUB") == "mm":
                        continue
                    ctx.wait(V, "mm")
                    nrm1 = scr[:, 1952:1960]
                    ctx.inc(V.tensor_reduce(nrm1, s_ps[:, :, :], axis=AX.X, op=ALU.max, negate=True), "ev")
                    if os.environ.get("L0A_SUB") == "nrm1":
                        continue
                    ctx.wait(V, "ev")
                    ctx.wait(V, "sc")
                    epre = scr[:, 0:1024].rearrange("p (h t) -> p h t", t=CH)
                    nrm1e = nrm1.broadcast_to([128, 8, CH])
                    ctx.inc(V.tensor_tensor(epre, s_ps[:, :, :], nrm1e, ALU.add), "ev")
                    if os.environ.get("L0A_SUB") == "pre":
                        continue
                    ctx.wait(SC, "ev")
                    ctx.inc(SC.activation(EB[:, :, :], epre, AF.Exp), "sc")
                    S1 = scr[:, 1960:1968]
                    ctx.wait(V, "sc")
                    ctx.inc(V.tensor_reduce(S1, EB[:, :, :], axis=AX.X, op=ALU.add), "ev")
                    ctx.wait(V, "ev")
                    ctx.inc(V.reciprocal(S1, S1), "ev")
                    ctx.wait(V, "ev")
                    dinv1 = S1.broadcast_to([128, 8, CH])
                    ctx.inc(V.tensor_tensor(EB[:, :, :], EB[:, :, :], dinv1, ALU.mult), "ev")
                    dinv2 = S2[:, 8 * hh:8 * hh + 8].broadcast_to([128, 8, 16])
                    ctx.wait(V, "ev")
                    ctx.inc(V.tensor_tensor(E2B[:, 8 * hh:8 * hh + 8, :],
                                            E2B[:, 8 * hh:8 * hh + 8, :], dinv2, ALU.mult), "ev")
                    ctx.wait(PE, "ev")
                    if os.environ.get("L0A_NOTR"):
                        continue
                    for i in range(8):
                        h = hh * 8 + i
                        sl = i % 2
                        PE.wait_ge(ctx.sem["ev"], ctx.tps_ev[sl])
                        ctx.inc(PE.transpose(t_ps[:, sl, :], EB[:, i, :], ident[:, :]), "mm")
                        V.wait_ge(ctx.sem["mm"], ctx.val["mm"])
                        ctx.inc(V.tensor_copy(ETB[:, i, :], t_ps[:, sl, :]), "ev")
                        PE.wait_ge(ctx.sem["ev"], ctx.val["ev"])
                        ctx.inc(PE.transpose(t2_ps[:, sl, :], E2B[:, h, :], ident[:, :]), "mm")
                        V.wait_ge(ctx.sem["mm"], ctx.val["mm"])
                        ctx.inc(V.tensor_copy(E2TB[0:16, h, :], t2_ps[:, sl, :]), "ev")
                        ctx.tps_ev[sl] = ctx.val["ev"]
                    for i in range(8):
                        h = hh * 8 + i
                        sl = i % 2
                        fcl = slice(64 * h, 64 * h + 64)
                        PE.wait_ge(ctx.sem["ev"], ctx.ops_ev[sl])
                        PE.matmul(o_ps[0:64, sl, :], VT[:, c, fcl], ETB[:, i, :],
                                  start=True, stop=False)
                        ctx.inc(PE.matmul(o_ps[0:64, sl, :], VM[0:16, fcl], E2TB[0:16, h, :],
                                          start=False, stop=True), "mm")
                        V.wait_ge(ctx.sem["mm"], ctx.val["mm"])
                        ctx.inc(V.tensor_copy(AO[hrows(h), h // 2, csl], o_ps[0:64, sl, :]), "ev")
                        ctx.ops_ev[sl] = ctx.val["ev"]
        if os.environ.get("DBG_L0A"):
            da = nc.dram_tensor("dbg_ao", [D, T], BF16, kind="ExternalOutput").ap()
            dkm = nc.dram_tensor("dbg_km", [D, NCH], BF16, kind="ExternalOutput").ap()
            dvm = nc.dram_tensor("dbg_vm", [16, D], BF16, kind="ExternalOutput").ap()
            nc.sync.wait_ge(ctx.sem["ev"], ctx.val["ev"])
            ctx.inc(nc.sync.dma_start(da.rearrange("(c p) t -> p c t", p=128), AO[:, :, :]), "st", 16)
            ctx.inc(nc.sync.dma_start(dkm.rearrange("(c p) j -> p c j", p=128), KM[:, :, :]), "st", 16)
            ctx.inc(nc.sync.dma_start(dvm, VM[0:16, :]), "st", 16)
            nc.sync.wait_ge(ctx.sem["st"], ctx.val["st"])
        if os.environ.get("L0A_TRUNC") == "B":
            return
        ctx.barrier()
        ctx.inc(nc.sync.dma_start(X[:, :, :], xsp.rearrange("(c p) t -> p c t", p=128)), "ldx", 16)
        nc.sync.wait_ge(ctx.sem["ldx"], ctx.val["ldx"])
        ctx.barrier()
        with contextlib.ExitStack() as esC:
            ctx.psum("proj_ps", [128, 6, 512], F32, scope=esC)
            ctx.sbuf("wbuf", [128, 2, 8, 512], BF16, scope=esC)
            emit_proj(ctx, "L0a_wo", AO, 8, T, [128] * 8, bias_row="L0a_bor",
                      resid=X, resid_scale=1.0, wait_src=("ev",))
        ctx.wait(V, "ev")
        ctx.wait(SC, "ev")
        ctx.wait(PE, "ev")


def prep_l0_attn(inputs, wmap):
    for nm, key in [("wq", "L0_attn_wq"), ("wk", "L0_attn_wk"),
                    ("wv", "L0_attn_wv"), ("wo", "L0_attn_wo")]:
        wmap["L0a_" + nm] = _t(inputs[key])
    wmap["L0a_bqr"] = _row(inputs["L0_attn_bq"])
    wmap["L0a_bkr"] = _row(inputs["L0_attn_bk"])
    wmap["L0a_bvr"] = _row(inputs["L0_attn_bv"])
    wmap["L0a_bor"] = _row(inputs["L0_attn_bo"])



TWIST_SRC = [4, 5, 6, 7, 0, 1, 2, 3]
TWIST_SGN = [+1, +1, -1, -1, -1, -1, +1, +1]


def emit_l0_ffn(ctx):
    nc = ctx.nc
    V, SC, PE, G = nc.vector, nc.scalar, nc.tensor, nc.gpsimd
    X, XH = ctx.sb["X"], ctx.sb["XH"]
    es = contextlib.ExitStack()
    with es:
        w1 = load_weight_resident(ctx, "L0f_w1", "l0f_w1sb", 8, 1024, es)
        w2 = load_weight_resident(ctx, "L0f_w2", "l0f_w2sb", 8, 512, es)
        w3 = load_weight_resident(ctx, "L0f_w3", "l0f_w3sb", 4, 256, es)
        w4 = load_weight_resident(ctx, "L0f_w4", "l0f_w4sb", 2, 1, es)
        g1 = load_weight_resident(ctx, "L0f_gw1", "l0f_g1sb", 8, 512, es)
        g2 = load_weight_resident(ctx, "L0f_gw2", "l0f_g2sb", 4, 256, es)
        g3 = load_weight_resident(ctx, "L0f_gw3", "l0f_g3sb", 2, 1, es)
        for rn in ["L0f_b1r", "L0f_b2r", "L0f_b3r", "L0f_b4r",
                   "L0f_gb1r", "L0f_gb2r", "L0f_gb3r"]:
            load_row(ctx, rn, es)
        ctx.psum("ln_ps", [128, 2, TB], F32, scope=es)
        ctx.psum("proj_ps", [128, 6, 512], F32, scope=es)
        OUT = ctx.sbuf("l0f_out", [128, DC, TB], F32, es)
        OUTB = ctx.sbuf("l0f_outb", [128, DC, TB], BF16, es)
        TMP = ctx.sbuf("l0f_tmp", [128, DC, TB], F32, es)
        H1 = ctx.sbuf("l0f_h1", [128, 8, TB], BF16, es)
        H2 = ctx.sbuf("l0f_h2", [128, 4, TB], BF16, es)
        H3 = ctx.sbuf("l0f_h3", [128, 2, TB], BF16, es)
        GC = ctx.sbuf("l0f_gc", [128, DC, 1], BF16, es)
        H1G = ctx.sbuf("l0f_h1g", [128, 4, 1], BF16, es)
        H2G = ctx.sbuf("l0f_h2g", [128, 2, 1], BF16, es)
        SMALL = ctx.sbuf("l0f_small", [1, 8], F32, es)
        TF = ctx.sbuf("l0f_tf", [1, TB], F32, es)
        CPL = ctx.sbuf("l0f_cpl", [1, TB], F32, es)

        CB = ctx.sb["ln_ps"][:, 1, :]
        GF, AR, S0, S1, SCR = (SMALL[0:1, i:i + 1] for i in range(5))
        ctx.inc(G.dma_start(SMALL[0:1, 1:2], ctx.dram["L0f_ar"]), "ldw", 16)

        emit_ln(ctx, X, XH, 0, T)

        for ci in range(DC):
            ctx.wait(V, "ev")
            ctx.inc(V.tensor_reduce(ctx.sb["scr"][:, 0:1], XH[:, ci, :], axis=AX.X, op=ALU.add), "ev")
            ctx.wait(V, "ev")
            ctx.inc(V.tensor_scalar(GC[:, ci, :], ctx.sb["scr"][:, 0:1], 1.0 / T, None, ALU.mult), "ev")

        ctx.wait(PE, "ldw")
        ctx.wait(PE, "ev")
        emit_proj(ctx, g1, GC, 8, 1, [128] * 4, bias_row="L0f_gb1r", act=AF.Gelu, dst=H1G)
        emit_proj(ctx, g2, H1G, 4, 1, [128] * 2, bias_row="L0f_gb2r", act=AF.Gelu, dst=H2G,
                  wait_src=("ev",))
        def gf_evac(m, n, nsz, bank, mo):
            SC.wait_ge(ctx.sem["mm"], ctx.val["mm"])
            ctx.inc(SC.activation(GF, ctx.sb["proj_ps"][0:1, bank, 0:1], AF.Sigmoid), "ev")
        emit_proj(ctx, g3, H2G, 2, 1, [1], bias_row="L0f_gb3r", evac=gf_evac, wait_src=("ev",))

        ctx.wait(V, "ev")
        ctx.wait(V, "ldw")
        ctx.inc(V.tensor_tensor(SCR, AR, GF, ALU.mult), "ev")
        ctx.wait(V, "ev")
        ctx.inc(V.tensor_scalar(S0, SCR, 1.4, None, ALU.mult), "ev")
        ctx.wait(V, "ev")
        ctx.inc(V.tensor_tensor(S0, S0, AR, ALU.subtract), "ev")
        ctx.wait(V, "ev")
        ctx.inc(V.tensor_scalar(S0, S0, 0.1, None, ALU.add), "ev")
        ctx.inc(V.tensor_scalar(S1, AR, 0.6, None, ALU.mult), "ev")

        for tb in range(NTB):
            o = tb * TB
            emit_ln(ctx, X, XH, o, o + TB, out_f32=OUT, f32_t0=0)
            for r in range(3):
                src = XH if r == 0 else OUTB
                st0 = o if r == 0 else 0
                ctx.wait(PE, "sc")
                emit_proj(ctx, w1, src, 8, TB, [128] * 8, bias_row="L0f_b1r",
                          act=AF.Gelu, dst=H1, src_t0=st0, wait_src=("ev",))
                emit_proj(ctx, w2, H1, 8, TB, [128] * 4, bias_row="L0f_b2r",
                          act=AF.Gelu, dst=H2, wait_src=("ev",))
                emit_proj(ctx, w3, H2, 4, TB, [128] * 2, bias_row="L0f_b3r",
                          act=AF.Gelu, dst=H3, wait_src=("ev",))

                def tf_evac(m, n, nsz, bank, mo):
                    SC.wait_ge(ctx.sem["mm"], ctx.val["mm"])
                    ctx.inc(SC.activation(TF[0:1, 0:nsz], ctx.sb["proj_ps"][0:1, bank, 0:nsz],
                                          AF.Sigmoid), "ev")
                emit_proj(ctx, w4, H3, 2, TB, [1], bias_row="L0f_b4r", evac=tf_evac,
                          wait_src=("ev",))
                ctx.wait(V, "ev")
                ctx.inc(V.tensor_scalar(CPL[0:1, :], TF[0:1, :], S1, S0, ALU.mult, ALU.add), "ev")
                ctx.wait(PE, "ev")
                ctx.inc(PE.matmul(CB, ctx.sb["c_ones1f"][0:1, :], CPL[0:1, :],
                                  start=True, stop=True), "mm")
                ctx.wait(V, "mm")
                for c in range(DC):
                    ctx.inc(V.tensor_tensor(TMP[:, c, :], CB, OUT[:, TWIST_SRC[c], :], ALU.mult), "ev")
                ctx.wait(V, "ev")
                last_cycle = (r == 2)
                for c in range(DC):
                    op = ALU.add if TWIST_SGN[c] > 0 else ALU.subtract
                    ctx.inc(V.tensor_tensor(OUT[:, c, :], OUT[:, c, :], TMP[:, c, :], op), "ev")
                if not last_cycle:
                    ctx.wait(SC, "ev")
                    for c in range(DC):
                        ctx.inc(SC.copy(OUTB[:, c, :], OUT[:, c, :]), "sc")
                ctx.wait(PE, "ev")
            ctx.wait(V, "ev")
            for c in range(DC):
                xs = X[:, c, o:o + TB]
                ctx.inc(V.scalar_tensor_tensor(xs, OUT[:, c, :], 0.5, xs, ALU.mult, ALU.add), "ev")
            ctx.wait(SC, "ev")
            ctx.wait(PE, "ev")



def _emit_l1_build_M(ctx, MH, ML, es):
    nc = ctx.nc
    V, SC, PE, G = nc.vector, nc.scalar, nc.tensor, nc.gpsimd
    X = ctx.sb["X"]
    identf = ctx.sb["c_identf"]
    scr = ctx.sb["scr"]
    xsp = nc.dram_tensor("l1f_xspill", [D, T], F32).ap()
    ctx.inc(nc.sync.dma_start(xsp.rearrange("(c p) t -> p c t", p=128), X[:, :, :]), "st", 16)
    nc.sync.wait_ge(ctx.sem["st"], ctx.val["st"])
    ctx.barrier()
    pool = X[:, :, :].rearrange("p c t -> p (c t)").bitcast(BF16)
    QT = {}
    for i, nm in enumerate(["rh", "rl", "ih", "il"]):
        QT[nm] = pool[:, 8192 * i:8192 * (i + 1)].rearrange("p (c t) -> p c t", t=1024)
    with contextlib.ExitStack() as esQ:
        ctx.psum("proj_ps", [128, 6, 512], F32, scope=esQ)
        ps = ctx.sb["proj_ps"]
        t_ps = ctx.psum("l1f_tps", [128, 2, 512], F32, scope=esQ)
        qwbuf = ctx.sbuf("l1f_qwbuf", [128, 2, 8, 768], BF16, esQ)
        WR = load_weight_resident(ctx, "L1f_wr", "l1f_wrsb", 8, D, esQ)
        WI = load_weight_resident(ctx, "L1f_wi", "l1f_wisb", 8, D, esQ)
        ctx.wait(PE, "ldw")
        qbase = ctx.val["mm"]
        baseA, baseB = ctx.val["ldwA"], ctx.val["ldwB"]
        CNAMES = ["c_dftChi", "c_dftClo", "c_dftSphi", "c_dftSplo", "c_dftSnhi", "c_dftSnlo"]
        for m in range(DC):
            par = m % 2
            sem = "ldwA" if par == 0 else "ldwB"
            if m >= 2:
                G.wait_ge(ctx.sem["mm"], qbase + (m - 1) * 20)
            else:
                ctx.wait(G, "mm")
            for kk, cn in enumerate(CNAMES):
                ctx.inc(G.dma_start(qwbuf[:, par, 0:8, 128 * kk:128 * (kk + 1)],
                                    ctx.dram[cn][:, 128 * m:128 * (m + 1)].rearrange("(c p) m -> p c m", p=128)),
                        sem, 16)
        for m in range(DC):
            par = m % 2
            if par == 0:
                PE.wait_ge(ctx.sem["ldwA"], baseA + 96 * (m // 2 + 1))
            else:
                PE.wait_ge(ctx.sem["ldwB"], baseB + 96 * (m // 2 + 1))
            for n in range(2):
                for qi in range(2):
                    bank = ctx.val["mm"] % 6
                    PE.wait_ge(ctx.sem["ev"], ctx.bank_ev[bank])
                    B1 = WR if qi == 0 else WI
                    B2 = WI if qi == 0 else WR
                    soff = 256 if qi == 0 else 512
                    first = True
                    for woff, BB in [(0, B1), (128, B1), (soff, B2), (soff + 128, B2)]:
                        for c in range(DC):
                            inst = PE.matmul(ps[:, bank, :], qwbuf[:, par, c, woff:woff + 128],
                                             BB[:, c, 512 * n:512 * (n + 1)],
                                             start=first, stop=(woff == soff + 128 and c == DC - 1))
                            first = False
                    ctx.inc(inst, "mm")
                    V.wait_ge(ctx.sem["mm"], ctx.val["mm"])
                    ctx.inc(V.tensor_copy(scr[:, 0:512], ps[:, bank, :]), "ev")
                    ctx.bank_ev[bank] = ctx.val["ev"]
                    ctx.wait(PE, "ev")
                    hi = QT["rh"] if qi == 0 else QT["ih"]
                    lo = QT["rl"] if qi == 0 else QT["il"]
                    for k in range(4):
                        sl = k % 2
                        oc = n * 4 + k
                        PE.wait_ge(ctx.sem["ev"], ctx.tps_ev[sl])
                        ctx.inc(PE.transpose(t_ps[:, sl, 0:128], scr[:, 128 * k:128 * (k + 1)],
                                             identf[:, :]), "mm")
                        V.wait_ge(ctx.sem["mm"], ctx.val["mm"])
                        ctx.inc(V.tensor_copy(hi[:, oc, 128 * m:128 * (m + 1)], t_ps[:, sl, 0:128]), "ev")
                        ctx.wait(V, "ev")
                        ctx.inc(V.tensor_tensor(lo[:, oc, 128 * m:128 * (m + 1)], t_ps[:, sl, 0:128],
                                                hi[:, oc, 128 * m:128 * (m + 1)], ALU.subtract), "ev")
                        ctx.tps_ev[sl] = ctx.val["ev"]
    with contextlib.ExitStack() as esM2:
        ctx.psum("proj_ps", [128, 6, 512], F32, scope=esM2)
        ps = ctx.sb["proj_ps"]
        mwbuf = ctx.sbuf("l1f_mwbuf", [128, 4, 8, 512], BF16, esM2)
        MNAMES = ["c_dftChi", "c_dftClo", "c_dftSnhi", "c_dftSnlo"]
        for n in range(2):
            ctx.wait(G, "mm")
            for kk, cn in enumerate(MNAMES):
                ctx.inc(G.dma_start(mwbuf[:, kk, :, :],
                                    ctx.dram[cn][:, 512 * n:512 * (n + 1)].rearrange("(c p) m -> p c m", p=128)),
                        "ldwA", 16)
            ctx.wait(PE, "ldwA")
            ctx.wait(PE, "ev")
            for m in range(DC):
                bank = ctx.val["mm"] % 6
                PE.wait_ge(ctx.sem["ev"], ctx.bank_ev[bank])
                terms = [(QT["rh"], 0), (QT["rh"], 1), (QT["rl"], 0),
                         (QT["ih"], 2), (QT["ih"], 3), (QT["il"], 2)]
                first = True
                for q, wk in terms:
                    for c in range(DC):
                        inst = PE.matmul(ps[:, bank, :], q[:, c, 128 * m:128 * (m + 1)],
                                         mwbuf[:, wk, c, :],
                                         start=first, stop=(q is QT["il"] and wk == 2 and c == DC - 1))
                        first = False
                ctx.inc(inst, "mm")
                V.wait_ge(ctx.sem["mm"], ctx.val["mm"])
                msl = slice(512 * n, 512 * (n + 1))
                ctx.inc(V.tensor_scalar(MH[:, m, msl], ps[:, bank, :],
                                        1.0 / 1024.0, None, ALU.mult), "ev")
                ctx.wait(V, "ev")
                ctx.inc(V.scalar_tensor_tensor(ML[:, m, msl], ps[:, bank, :], 1.0 / 1024.0,
                                               MH[:, m, msl], ALU.mult, ALU.subtract), "ev")
                ctx.bank_ev[bank] = ctx.val["ev"]
    ctx.barrier()
    ctx.inc(nc.sync.dma_start(X[:, :, :], xsp.rearrange("(c p) t -> p c t", p=128)), "ldx", 16)
    nc.sync.wait_ge(ctx.sem["ldx"], ctx.val["ldx"])
    ctx.barrier()


def emit_l1_ffn(ctx):
    nc = ctx.nc
    V, SC, PE = nc.vector, nc.scalar, nc.tensor
    X, XH = ctx.sb["X"], ctx.sb["XH"]
    es = contextlib.ExitStack()
    if not hasattr(ctx, "cache"):
        ctx.cache = {}
    with es:
        MH = ctx.sbuf("l1f_Mh", [128, DC, D], BF16, es)
        ML = ctx.sbuf("l1f_Ml", [128, DC, D], BF16, es)
        load_row(ctx, "L1f_br", es)
        if "l1f_Mcache" in ctx.cache:
            ctx.inc(nc.sync.dma_start(MH[:, :, :], ctx.cache["l1f_Mcache"]), "ldx", 16)
            ctx.inc(nc.sync.dma_start(ML[:, :, :], ctx.cache["l1f_Mcache_lo"]), "ldx", 16)
            nc.sync.wait_ge(ctx.sem["ldx"], ctx.val["ldx"])
            ctx.barrier()
        else:
            _emit_l1_build_M(ctx, MH, ML, es)
            mc = nc.dram_tensor("l1f_Mcache", [128, DC, D], BF16).ap()
            mcl = nc.dram_tensor("l1f_Mcache_lo", [128, DC, D], BF16).ap()
            ctx.cache["l1f_Mcache"] = mc
            ctx.cache["l1f_Mcache_lo"] = mcl
            ctx.wait(nc.sync, "ev")
            ctx.inc(nc.sync.dma_start(mc, MH[:, :, :]), "st", 16)
            ctx.inc(nc.sync.dma_start(mcl, ML[:, :, :]), "st", 16)
            nc.sync.wait_ge(ctx.sem["st"], ctx.val["st"])
            ctx.barrier()
        with contextlib.ExitStack() as esP:
            ctx.psum("ln_ps", [128, 2, TB], F32, scope=esP)
            ctx.psum("proj_ps", [128, 6, 512], F32, scope=esP)
            emit_ln(ctx, X, XH, 0, T)
            emit_proj(ctx, [MH, ML], XH, DC, T, [128] * 8, bias_row="L1f_br",
                      resid=X, resid_scale=0.5, wait_src=("ev",))
        ctx.wait(V, "ev")
        ctx.wait(SC, "ev")
        ctx.wait(PE, "ev")



def emit_l1_attn(ctx):
    nc = ctx.nc
    V, SC, PE, G = nc.vector, nc.scalar, nc.tensor, nc.gpsimd
    X, XH = ctx.sb["X"], ctx.sb["XH"]
    NH, HD = 16, 64
    NTC = T // 128
    es = contextlib.ExitStack()
    with es:
        pool = X[:, :, :].rearrange("p c t -> p (c t)").bitcast(BF16)
        QTok = pool[:, 0:16384].rearrange("p (tc f) -> p tc f", f=D)
        KTok = pool[:, 16384:32768].rearrange("p (tc f) -> p tc f", f=D)
        KFfm = pool[:, 0:16384].rearrange("p (c t) -> p c t", t=T)
        AOfm = pool[:, 16384:32768].rearrange("p (c t) -> p c t", t=T)
        QFfm = XH
        VTok = ctx.sbuf("l1a_vt", [128, NTC, D], BF16, es)
        DELTA = VTok[:, :, :].rearrange("p tc f -> p (tc f)").bitcast(F32).rearrange(
            "p (c t) -> p c t", t=T)
        for rn in ["L1a_bqr", "L1a_bkr", "L1a_bvr", "L1a_bor",
                   "c_j16", "c_posf", "c_onesq"]:
            load_row(ctx, rn, es)
        NF = ctx.sbuf("l1a_nf", [1, 4], F32, es)
        IVF = ctx.sbuf("l1a_ivf", [1, 16], F32, es)
        NCS = ctx.sbuf("l1a_ncs", [128, 2, 16], F32, es)
        QN = ctx.sbuf("l1a_qn", [128, 16], F32, es)
        KN = ctx.sbuf("l1a_kn", [128, 16], F32, es)
        NCB = ctx.sbuf("l1a_ncb", [128, 16], F32, es)
        SM = ctx.sbuf("l1a_sm", [16, 4], F32, es)
        SMR = ctx.sbuf("l1a_smr", [1, 16], F32, es)
        QS = ctx.sbuf("l1a_qs", [64, 2048], BF16, es)
        KS = ctx.sbuf("l1a_ks", [64, 2048], BF16, es)
        ET = ctx.sbuf("l1a_et", [128, 4, 512], BF16, es)
        IDN = ctx.sbuf("l1a_idn", [64, 1024], F32, es)
        ctx.inc(G.dma_start(NF[0:1, 0:1], ctx.dram["L1a_nf"]), "ldw", 16)
        scr = ctx.sb["scr"]
        ident = ctx.sb["c_ident"]
        identf = ctx.sb["c_identf"]
        xsp = nc.dram_tensor("l1a_xspill", [D, T], F32).ap()

        with contextlib.ExitStack() as esA:
            ctx.psum("ln_ps", [128, 2, TB], F32, scope=esA)
            ctx.psum("proj_ps", [128, 6, 512], F32, scope=esA)
            ctx.sbuf("wbuf", [128, 2, 8, 512], BF16, scope=esA)
            emit_ln(ctx, X, XH, 0, T)
            ctx.inc(nc.sync.dma_start(xsp.rearrange("(c p) t -> p c t", p=128), X[:, :, :]), "st", 16)
            nc.sync.wait_ge(ctx.sem["st"], ctx.val["st"])
            ctx.barrier()
            emit_proj_tok(ctx, "L1a_wq", XH, 8, D, QTok, bias_row="L1a_bqr", wait_src=("ev",))
            emit_proj_tok(ctx, "L1a_wk", XH, 8, D, KTok, bias_row="L1a_bkr")
            emit_proj_tok(ctx, "L1a_wv", XH, 8, D, VTok, bias_row="L1a_bvr")

            ctx.wait(SC, "ldw")
            ctx.inc(SC.activation(NF[0:1, 1:2], NF[0:1, 0:1], AF.Sigmoid), "sc")
            ctx.wait(SC, "sc")
            ctx.inc(SC.activation(NF[0:1, 2:3], NF[0:1, 1:2], AF.Ln,
                                  scale=np.float32(3400.0), bias=np.float32(8300.0)), "sc")
            ctx.wait(V, "sc")
            ctx.inc(V.tensor_scalar(IVF[0:1, :], ctx.sb["c_j16"][0:1, :], NF[0:1, 2:3], None, ALU.mult), "ev")
            ctx.wait(SC, "ev")
            ctx.inc(SC.activation(IVF[0:1, :], IVF[0:1, :], AF.Exp, scale=np.float32(-1.0)), "sc")
            ctx.wait(V, "sc")
            ctx.inc(V.tensor_scalar(IVF[0:1, :], IVF[0:1, :], float(2.0 * np.pi / 2047.0), None, ALU.mult), "ev")

            rope_ps = ctx.sb["ln_ps"]
            for tc in range(NTC):
                ctx.wait(PE, "ev")
                ctx.wait(PE, "sc")
                ctx.inc(PE.matmul(rope_ps[:, 0, 0:16], ctx.sb["c_posf"][0:1, 128 * tc:128 * (tc + 1)],
                                  IVF[0:1, :], start=True, stop=True), "mm")
                ctx.wait(SC, "mm")
                ctx.inc(SC.activation(NCS[:, 1, :], rope_ps[:, 0, 0:16], AF.Sin,
                                      bias=np.float32(-np.pi)), "sc")
                ctx.inc(SC.activation(NCS[:, 0, :], rope_ps[:, 0, 0:16], AF.Abs,
                                      bias=np.float32(-np.pi)), "sc")
                ctx.wait(SC, "sc")
                ctx.inc(SC.activation(NCS[:, 0, :], NCS[:, 0, :], AF.Sin,
                                      scale=np.float32(-1.0), bias=np.float32(np.pi / 2.0)), "sc")
                ncb = NCS[:, 0, :].unsqueeze(1).broadcast_to([128, 16, 16])
                nsb = NCS[:, 1, :].unsqueeze(1).broadcast_to([128, 16, 16])
                for QK in (QTok, KTok):
                    hv = QK[:, tc, :].rearrange("p (h d) -> p h d", d=64)
                    x1 = hv[:, :, 32:48]
                    x2 = hv[:, :, 48:64]
                    t1 = scr[:, 0:256].rearrange("p (h j) -> p h j", j=16)
                    t2 = scr[:, 256:512].rearrange("p (h j) -> p h j", j=16)
                    t3 = scr[:, 512:768].rearrange("p (h j) -> p h j", j=16)
                    t4 = scr[:, 768:1024].rearrange("p (h j) -> p h j", j=16)
                    ctx.wait(V, "sc")
                    ctx.wait(V, "ev")
                    ctx.inc(V.tensor_tensor(t1, x2, nsb, ALU.mult), "ev")
                    ctx.inc(V.tensor_tensor(t2, x1, ncb, ALU.mult), "ev")
                    ctx.inc(V.tensor_tensor(t3, x1, nsb, ALU.mult), "ev")
                    ctx.inc(V.tensor_tensor(t4, x2, ncb, ALU.mult), "ev")
                    ctx.wait(V, "ev")
                    ctx.inc(V.tensor_tensor(x1, t1, t2, ALU.subtract), "ev")
                    ctx.inc(V.tensor_tensor(t3, t3, t4, ALU.add), "ev")
                    ctx.wait(V, "ev")
                    ctx.inc(V.tensor_scalar(x2, t3, -1.0, None, ALU.mult), "ev")
            for QK, NN in ((QTok, QN), (KTok, KN)):
                ctx.wait(V, "ev")
                ctx.inc(V.memset(NN[:, :], 0.0), "ev")
                for tc in range(NTC):
                    ctx.wait(SC, "ev")
                    ctx.inc(SC.activation(scr[:, 0:1024], QK[:, tc, :], AF.Square), "sc")
                    ctx.wait(V, "sc")
                    ctx.inc(V.tensor_reduce(scr[:, 1024:1040],
                                            scr[:, 0:1024].rearrange("p (h d) -> p h d", d=64),
                                            axis=AX.X, op=ALU.add), "ev")
                    ctx.wait(V, "ev")
                    ctx.inc(V.tensor_tensor(NN[:, :], NN[:, :], scr[:, 1024:1040], ALU.max), "ev")
                    ctx.wait(SC, "ev")
        ctx.wait(PE, "ev")
        ctx.wait(SC, "ev")
        with contextlib.ExitStack() as esT:
            tp = ctx.psum("l1a_tp", [128, 512], F32, scope=esT)
            ctx.inc(PE.transpose(tp[0:16, 0:128], QN[:, :], identf[:, :]), "mm")
            ctx.wait(V, "mm")
            ctx.inc(V.tensor_reduce(SM[0:16, 0:1], tp[0:16, 0:128], axis=AX.X, op=ALU.max), "ev")
            ctx.wait(PE, "ev")
            ctx.inc(PE.transpose(tp[0:16, 0:128], KN[:, :], identf[:, :]), "mm")
            ctx.wait(V, "mm")
            ctx.inc(V.tensor_reduce(SM[0:16, 1:2], tp[0:16, 0:128], axis=AX.X, op=ALU.max), "ev")
            ctx.wait(SC, "ev")
            ctx.inc(SC.activation(SM[0:16, 2:3], SM[0:16, 0:1], AF.Sqrt), "sc")
            ctx.inc(SC.activation(SM[0:16, 3:4], SM[0:16, 1:2], AF.Sqrt), "sc")
            ctx.wait(V, "sc")
            ctx.inc(V.tensor_tensor(SM[0:16, 2:3], SM[0:16, 2:3], SM[0:16, 3:4], ALU.mult), "ev")
            ctx.wait(V, "ev")
            ctx.inc(V.tensor_scalar(SM[0:16, 2:3], SM[0:16, 2:3], -1.01 / 8.0, None, ALU.mult), "ev")
            ctx.wait(PE, "ev")
            ctx.inc(PE.transpose(tp[0:1, 0:16], SM[0:16, 2:3], identf[0:16, 0:16]), "mm")
            ctx.wait(V, "mm")
            ctx.inc(V.tensor_copy(SMR[0:1, :], tp[0:1, 0:16]), "ev")
            ctx.wait(PE, "ev")
            ctx.inc(PE.matmul(tp[:, 0:16], ctx.sb["c_ones1f"][0:1, :], SMR[0:1, :],
                              start=True, stop=True), "mm")
            ctx.wait(V, "mm")
            ctx.inc(V.tensor_copy(NCB[:, :], tp[:, 0:16]), "ev")
        with contextlib.ExitStack() as esT2:
            t_ps = ctx.psum("l1a_t2", [128, 2, 1024], BF16, scope=esT2)
            for src, dst in ((QTok, QFfm), (KTok, KFfm)):
                ctx.wait(PE, "ev")
                for tc in range(NTC):
                    for fc in range(DC):
                        sl = (tc * DC + fc) % 2
                        PE.wait_ge(ctx.sem["ev"], ctx.tps_ev[sl])
                        ctx.inc(PE.transpose(t_ps[:, sl, 0:128], src[:, tc, 128 * fc:128 * (fc + 1)],
                                             ident[:, :]), "mm")
                        V.wait_ge(ctx.sem["mm"], ctx.val["mm"])
                        ctx.inc(V.tensor_copy(dst[:, fc, 128 * tc:128 * (tc + 1)], t_ps[:, sl, 0:128]), "ev")
                        ctx.tps_ev[sl] = ctx.val["ev"]
        with contextlib.ExitStack() as esB:
            s_ps = ctx.psum("l1a_sps", [128, 4, 512], F32, scope=esB)
            av_ps = ctx.psum("l1a_avps", [64, 2, 512], F32, scope=esB)
            dn_ps = ctx.psum("l1a_dnps", [64, 2, 512], F32, scope=esB)
            onesq = ctx.sb["c_onesq"]
            sps_ev = [ctx.val["ev"]] * 4
            et_mm = [ctx.val["mm"]] * 4
            for h in range(NH):
                hc, hr = h // 2, (h % 2) * 64
                if h % 2 == 1:
                    ctx.wait(V, "mm")
                    ctx.inc(V.tensor_copy(QS[:, :], QFfm[64:128, hc, :]), "ev")
                    ctx.wait(V, "ev")
                    ctx.inc(V.tensor_copy(KS[:, :], KFfm[64:128, hc, :]), "ev")
                    qh = QS[:, :]
                    kh = KS[:, :]
                else:
                    qh = QFfm[0:64, hc, :]
                    kh = KFfm[0:64, hc, :]
                for qh2 in range(2):
                    ctx.wait(PE, "ev")

                    def avdn(kc, h=h):
                        for n in range(2):
                            sl = (kc * 2 + n) % 4
                            PE.wait_ge(ctx.sem["ev"], sps_ev[sl])
                            PE.matmul(av_ps[0:64, n, :], VTok[:, kc, 64 * h:64 * (h + 1)],
                                      ET[:, sl, :], start=(kc == 0), stop=(kc == NTC - 1))
                            inst = PE.matmul(dn_ps[0:64, n, :], onesq[:, 0:64],
                                             ET[:, sl, :], start=(kc == 0), stop=(kc == NTC - 1))
                            ctx.inc(inst, "mm")
                            et_mm[sl] = ctx.val["mm"]

                    for kc in range(NTC):
                        for n in range(2):
                            nq = slice(1024 * qh2 + 512 * n, 1024 * qh2 + 512 * (n + 1))
                            sl = (kc * 2 + n) % 4
                            PE.wait_ge(ctx.sem["ev"], sps_ev[sl])
                            ctx.inc(PE.matmul(s_ps[:, sl, :], kh[:, 128 * kc:128 * (kc + 1)],
                                              qh[:, nq], start=True, stop=True), "mm")
                            SC.wait_ge(ctx.sem["mm"], ctx.val["mm"])
                            SC.wait_ge(ctx.sem["mm"], et_mm[sl])
                            ctx.inc(SC.activation(ET[:, sl, :], s_ps[:, sl, :], AF.Exp,
                                                  bias=NCB[:, h:h + 1], scale=np.float32(0.125)), "ev")
                            sps_ev[sl] = ctx.val["ev"]
                        if kc > 0:
                            avdn(kc - 1)
                    avdn(NTC - 1)
                    ctx.wait(SC, "mm")
                    for n in range(2):
                        ctx.inc(SC.activation(IDN[0:64, 512 * n:512 * (n + 1)], dn_ps[0:64, n, :],
                                              AF.Ln), "sc")
                    ctx.wait(SC, "sc")
                    for n in range(2):
                        ctx.inc(SC.activation(IDN[0:64, 512 * n:512 * (n + 1)],
                                              IDN[0:64, 512 * n:512 * (n + 1)],
                                              AF.Exp, scale=np.float32(-1.0)), "sc")
                    ctx.wait(V, "sc")
                    for n in range(2):
                        nq = slice(1024 * qh2 + 512 * n, 1024 * qh2 + 512 * (n + 1))
                        ctx.inc(V.tensor_tensor(AOfm[hr:hr + 64, hc, nq], av_ps[0:64, n, :],
                                                IDN[0:64, 512 * n:512 * (n + 1)], ALU.mult), "ev")
                    ctx.wait(PE, "ev")
        ctx.barrier()
        ctx.inc(nc.sync.dma_start(X[:, 0:4, :], xsp.rearrange("(c p) t -> p c t", p=128)[:, 0:4, :]),
                "ldx", 16)
        nc.sync.wait_ge(ctx.sem["ldx"], ctx.val["ldx"])
        ctx.barrier()
        with contextlib.ExitStack() as esC:
            ctx.psum("proj_ps", [128, 6, 512], F32, scope=esC)
            ctx.sbuf("wbuf", [128, 2, 8, 512], BF16, scope=esC)
            ps = ctx.sb["proj_ps"]

            def oevac(m, n, nsz, bank, mo):
                psl = ps[:, bank, 0:nsz]
                V.wait_ge(ctx.sem["mm"], ctx.val["mm"])
                if m < 4:
                    rsl = X[:, m, 512 * n:512 * n + nsz]
                    ctx.inc(V.scalar_tensor_tensor(rsl, psl, 1.0, rsl, ALU.mult, ALU.add), "ev")
                else:
                    ctx.inc(V.tensor_copy(DELTA[:, m - 4, 512 * n:512 * n + nsz], psl), "ev")
                ctx.bank_ev[bank] = ctx.val["ev"]

            emit_proj(ctx, "L1a_wo", AOfm, 8, T, [128] * 8, bias_row="L1a_bor",
                      evac=oevac, wait_src=("ev",))
        ctx.wait(nc.sync, "ev")
        ctx.inc(nc.sync.dma_start(X[:, 4:8, :], xsp.rearrange("(c p) t -> p c t", p=128)[:, 4:8, :]),
                "ldx", 16)
        nc.sync.wait_ge(ctx.sem["ldx"], ctx.val["ldx"])
        ctx.barrier()
        for c in range(4):
            xs = X[:, 4 + c, :]
            ctx.inc(V.tensor_tensor(xs, xs, DELTA[:, c, :], ALU.add), "ev")
        ctx.wait(V, "ev")
        ctx.wait(SC, "ev")
        ctx.wait(PE, "ev")



def emit_l2_ffn(ctx):
    nc = ctx.nc
    V, SC, PE, G = nc.vector, nc.scalar, nc.tensor, nc.gpsimd
    X, XH = ctx.sb["X"], ctx.sb["XH"]
    NE = 8
    es = contextlib.ExitStack()
    with es:
        ROUT = X
        scr = ctx.sb["scr"]
        xsp = ctx.dscratch("l2f_xspill", [D, T], F32)
        CW3 = ctx.sbuf("l2f_cw3", [128, 16, NE], F32, es)
        CWROW = ctx.sbuf("l2f_cwrow", [1, T], F32, es)
        CWBC = ctx.sbuf("l2f_cwbc", [128, 16, 128], BF16, es)
        SIL = ctx.sbuf("l2f_sil", [128, 4, TB], BF16, es)
        T2B = ctx.sbuf("l2f_t2", [128, 4, TB], BF16, es)
        ACT = ctx.sbuf("l2f_act", [128, 4, TB], BF16, es)

        with contextlib.ExitStack() as esR:
            ctx.psum("ln_ps", [128, 2, TB], F32, scope=esR)
            rps = ctx.psum("l2f_rps", [128, 2, 512], F32, scope=esR)
            RF32 = ctx.sbuf("l2f_rf32", [128, DC, TB], F32, esR)
            RWf = ctx.sbuf("l2f_rwf", [128, DC, NE], F32, esR)
            ctx.inc(nc.sync.dma_start(RWf[:, :, :],
                                      ctx.dram["L2f_rwT"].rearrange("(c p) m -> p c m", p=128)), "ldx", 16)
            for tb in range(NTB):
                o = tb * TB
                emit_ln(ctx, X, XH, o, o + TB, out_f32=RF32)
                ctx.wait(PE, "ev")
                ctx.wait(PE, "ldx")
                for tcl in range(4):
                    bank = tcl % 2
                    PE.wait_ge(ctx.sem["ev"], ctx.rps_ev[bank])
                    for c in range(DC):
                        inst = PE.matmul(rps[:, bank, 0:NE], RF32[:, c, 128 * tcl:128 * (tcl + 1)],
                                         RWf[:, c, :], start=(c == 0), stop=(c == DC - 1))
                    ctx.inc(inst, "mm")
                    V.wait_ge(ctx.sem["mm"], ctx.val["mm"])
                    ctx.inc(V.tensor_copy(CW3[:, 4 * tb + tcl, :], rps[:, bank, 0:NE]), "ev")
                    ctx.rps_ev[bank] = ctx.val["ev"]
        with contextlib.ExitStack() as esA:
            ctx.psum("ln_ps", [128, 2, TB], F32, scope=esA)
            ctx.psum("proj_ps", [128, 6, 512], F32, scope=esA)
            ctx.sbuf("wbuf", [128, 2, 16, 512], BF16, scope=esA)
            GS = ctx.sbuf("l2f_gs", [1, T], F32, esA)
            GSB = ctx.sbuf("l2f_gsb", [128, TB], F32, esA)
            G16 = ctx.sbuf("l2f_g16", [128, 16, TB], BF16, esA)
            ASH = ctx.sbuf("l2f_ash", [128, 16, TB], BF16, esA)
            emit_ln(ctx, X, XH, 0, T)
            ctx.inc(nc.sync.dma_start(xsp.rearrange("(c p) t -> p c t", p=128), X[:, :, :]), "st", 16)
            nc.sync.wait_ge(ctx.sem["st"], ctx.val["st"])
            ctx.barrier()
            nrm = scr[:, 0:16]
            ctx.inc(V.tensor_reduce(nrm, CW3[:, :, :], axis=AX.X, op=ALU.max, negate=True), "ev")
            ctx.wait(V, "ev")
            nrme = nrm.broadcast_to([128, 16, NE])
            ctx.inc(V.tensor_tensor(CW3[:, :, :], CW3[:, :, :], nrme, ALU.add), "ev")
            ctx.wait(SC, "ev")
            ctx.inc(SC.activation(CW3[:, :, :], CW3[:, :, :], AF.Exp), "sc")
            ctx.wait(V, "sc")
            m1 = scr[:, 16:32]
            ctx.inc(V.tensor_reduce(m1, CW3[:, :, :], axis=AX.X, op=ALU.max), "ev")
            ctx.wait(V, "ev")
            mk1 = scr[:, 1024:1024 + 128].rearrange("p (a b) -> p a b", b=NE)
            ctx.inc(V.tensor_tensor(mk1, CW3[:, :, :], m1.broadcast_to([128, 16, NE]), ALU.is_ge), "ev")
            ctx.wait(V, "ev")
            p2 = scr[:, 1152:1152 + 128].rearrange("p (a b) -> p a b", b=NE)
            ctx.inc(V.scalar_tensor_tensor(p2, mk1, -1e30, CW3[:, :, :], ALU.mult, ALU.add), "ev")
            ctx.wait(V, "ev")
            m2 = scr[:, 32:48]
            ctx.inc(V.tensor_reduce(m2, p2, axis=AX.X, op=ALU.max), "ev")
            ctx.wait(V, "ev")
            s12 = scr[:, 48:64]
            ctx.inc(V.tensor_tensor(s12, m1, m2, ALU.add), "ev")
            mk2 = mk1
            ctx.inc(V.tensor_tensor(mk2, CW3[:, :, :], m2.broadcast_to([128, 16, NE]), ALU.is_ge), "ev")
            ctx.wait(V, "ev")
            ctx.inc(V.reciprocal(s12, s12), "ev")
            ctx.wait(V, "ev")
            ctx.inc(V.tensor_tensor(CW3[:, :, :], CW3[:, :, :], mk2, ALU.mult), "ev")
            ctx.wait(V, "ev")
            ctx.inc(V.tensor_tensor(CW3[:, :, :], CW3[:, :, :], s12.broadcast_to([128, 16, NE]), ALU.mult), "ev")

            def gs_evac(m, n, nsz, bank, mo):
                SC.wait_ge(ctx.sem["mm"], ctx.val["mm"])
                ctx.inc(SC.activation(GS[0:1, 512 * n:512 * n + nsz],
                                      ctx.sb["proj_ps"][0:1, bank, 0:nsz], AF.Sigmoid), "ev")
            emit_proj(ctx, "L2f_segT", XH, 8, T, [1], evac=gs_evac, wait_src=("ev",))
            ctx.wait(V, "ev")
            ctx.inc(V.tensor_scalar(GS[0:1, :], GS[0:1, :], 0.5, None, ALU.mult), "ev")

            for tb in range(NTB):
                o = tb * TB
                emit_proj(ctx, "L2f_sgT", XH, 8, TB, [128] * 16, act=AF.Silu,
                          dst=G16, src_t0=o, wait_src=("ev",))

                def u_evac(m, n, nsz, bank, mo):
                    V.wait_ge(ctx.sem["mm"], ctx.val["mm"])
                    V.wait_ge(ctx.sem["ev"], ctx.val["ev"])
                    ctx.inc(V.tensor_tensor(ASH[:, m, 0:nsz], ctx.sb["proj_ps"][:, bank, 0:nsz],
                                            G16[:, m, 0:nsz], ALU.mult), "ev")
                emit_proj(ctx, "L2f_suT", XH, 8, TB, [128] * 16, evac=u_evac, src_t0=o)
                ctx.wait(PE, "ev")
                bank0 = ctx.val["mm"] % 6
                PE.wait_ge(ctx.sem["ev"], ctx.bank_ev[bank0])
                ctx.inc(PE.matmul(ctx.sb["proj_ps"][:, bank0, :], ctx.sb["c_ones1f"][0:1, :],
                                  GS[0:1, o:o + TB], start=True, stop=True), "mm")
                V.wait_ge(ctx.sem["mm"], ctx.val["mm"])
                ctx.inc(V.tensor_copy(GSB[:, :], ctx.sb["proj_ps"][:, bank0, :]), "ev")
                ctx.bank_ev[bank0] = ctx.val["ev"]

                def sh_evac(m, n, nsz, bank, mo):
                    V.wait_ge(ctx.sem["mm"], ctx.val["mm"])
                    V.wait_ge(ctx.sem["ev"], ctx.val["ev"])
                    ctx.inc(V.tensor_tensor(ROUT[:, m, o:o + nsz], ctx.sb["proj_ps"][:, bank, 0:nsz],
                                            GSB[:, 0:nsz], ALU.mult), "ev")
                emit_proj(ctx, "L2f_sdT", ASH, 16, TB, [128] * 8, evac=sh_evac, wait_src=("ev",))

        if os.environ.get("L2F_DBG"):
            dcw = nc.dram_tensor("dbg_cw3", [128, 16, NE], F32, kind="ExternalOutput").ap()
            nc.sync.wait_ge(ctx.sem["ev"], ctx.val["ev"])
            ctx.inc(nc.sync.dma_start(dcw, CW3[:, :, :]), "st", 16)
            nc.sync.wait_ge(ctx.sem["st"], ctx.val["st"])
        NE_RUN = int(os.environ.get("L2F_NE", NE))
        ctx.barrier()
        with contextlib.ExitStack() as esE:
            ep = ctx.psum("l2f_ep", [128, 8, 512], F32, scope=esE)
            WG = ctx.sbuf("l2f_wg", [128, 2, 8, 1024], BF16, esE)
            WD = ctx.sbuf("l2f_wd", [128, 2, 4, 1024], BF16, esE)
            gup = ctx.dram["L2f_gupT"]
            dpt = ctx.dram["L2f_dpT"]
            ebank_ev = [ctx.val["ev"]] * 8

            def load_expert(e):
                par = e % 2
                sem = "ldwA" if par == 0 else "ldwB"
                ctx.wait(G, "mm")
                ctx.inc(G.dma_start(WG[:, par, :, :], gup[e].rearrange("(c p) m -> p c m", p=128)), sem, 16)
                ctx.inc(G.dma_start(WD[:, par, :, :], dpt[e].rearrange("(c p) m -> p c m", p=128)), sem, 16)

            if NE_RUN > 0:
                load_expert(0)
            if NE_RUN > 1:
                load_expert(1)
            for e in range(NE_RUN):
                par = e % 2
                ctx.wait(PE, "ev")
                for tg in range(4):
                    PE.wait_ge(ctx.sem["ev"], ebank_ev[0])
                    for k in range(4):
                        tc = tg * 4 + k
                        ctx.inc(PE.transpose(ep[0:1, 0, 128 * k:128 * (k + 1)],
                                             CW3[:, tc, e:e + 1], ctx.sb["c_identf"][:, :]), "mm")
                    V.wait_ge(ctx.sem["mm"], ctx.val["mm"])
                    ctx.inc(V.tensor_copy(CWROW[0:1, 512 * tg:512 * (tg + 1)], ep[0:1, 0, :]), "ev")
                    ebank_ev[0] = ctx.val["ev"]
                for tg in range(4):
                    PE.wait_ge(ctx.sem["ev"], ebank_ev[1])
                    ctx.wait(PE, "ev")
                    ctx.inc(PE.matmul(ep[:, 1, :], ctx.sb["c_ones1f"][0:1, :],
                                      CWROW[0:1, 512 * tg:512 * (tg + 1)], start=True, stop=True), "mm")
                    V.wait_ge(ctx.sem["mm"], ctx.val["mm"])
                    ctx.inc(V.tensor_copy(CWBC[:, 4 * tg:4 * tg + 4, :].rearrange("p a b -> p (a b)"),
                                          ep[:, 1, :]), "ev")
                    ebank_ev[1] = ctx.val["ev"]
                ctx.wait(PE, "ldwA" if par == 0 else "ldwB")
                for tb in range(NTB):
                    o = tb * TB
                    for mchunk in range(8):
                        PE.wait_ge(ctx.sem["ev"], ebank_ev[mchunk])
                        for c in range(DC):
                            inst = PE.matmul(ep[:, mchunk, :], WG[:, par, c, 128 * mchunk:128 * (mchunk + 1)],
                                             XH[:, c, o:o + TB], start=(c == 0), stop=(c == DC - 1))
                        ctx.inc(inst, "mm")
                    SC.wait_ge(ctx.sem["mm"], ctx.val["mm"])
                    V.wait_ge(ctx.sem["mm"], ctx.val["mm"])
                    for gc in range(4):
                        ctx.inc(SC.activation(SIL[:, gc, :], ep[:, gc, :], AF.Silu), "ev")
                        cwslice = CWBC[:, 4 * tb:4 * tb + 4, :].rearrange("p a b -> p (a b)")
                        ctx.inc(V.tensor_tensor(T2B[:, gc, :], ep[:, 4 + gc, :],
                                                cwslice.unsqueeze(1).broadcast_to([128, 1, 512]).rearrange("p o f -> p (o f)") if False else cwslice,
                                                ALU.mult), "ev")
                    ctx.wait(V, "ev")
                    for gc in range(4):
                        ctx.inc(V.tensor_tensor(ACT[:, gc, :], SIL[:, gc, :], T2B[:, gc, :], ALU.mult), "ev")
                    for b in range(8):
                        ebank_ev[b] = ctx.val["ev"]
                    for hh in range(2):
                        for hm in range(4):
                            bank = 4 * hh + hm
                            PE.wait_ge(ctx.sem["ev"], ebank_ev[bank])
                            for c in range(4):
                                inst = PE.matmul(ep[:, bank, :], WD[:, par, c, 128 * (4 * hh + hm):128 * (4 * hh + hm + 1)],
                                                 ACT[:, c, :], start=(c == 0), stop=(c == 3))
                            ctx.inc(inst, "mm")
                        V.wait_ge(ctx.sem["mm"], ctx.val["mm"])
                        for hm in range(4):
                            bank = 4 * hh + hm
                            rsl = ROUT[:, 4 * hh + hm, o:o + TB]
                            ctx.inc(V.scalar_tensor_tensor(rsl, ep[:, bank, :], 0.5, rsl,
                                                           ALU.mult, ALU.add), "ev")
                            ebank_ev[bank] = ctx.val["ev"]
                if e + 2 < NE_RUN:
                    load_expert(e + 2)
        ctx.barrier()
        ctx.inc(G.dma_start(X[:, :, :], xsp.rearrange("(c p) t -> p c t", p=128),
                            accum_op=ALU.add), "ldw", 16)
        G.wait_ge(ctx.sem["ldw"], ctx.val["ldw"])
        ctx.barrier()



def emit_l2_attn(ctx):
    nc = ctx.nc
    V, SC, PE, G = nc.vector, nc.scalar, nc.tensor, nc.gpsimd
    X, XH = ctx.sb["X"], ctx.sb["XH"]
    NH, HD, FD = 16, 64, 4
    NTC = T // 128
    es = contextlib.ExitStack()
    with es:
        pool = X[:, :, :].rearrange("p c t -> p (c t)").bitcast(BF16)
        QF = pool[:, 0:16384].rearrange("p (c t) -> p c t", t=T)
        KF = pool[:, 16384:32768].rearrange("p (c t) -> p c t", t=T)
        UP2 = pool[:, 0:16384].rearrange("p (c t) -> p c t", t=T)
        VTok = ctx.sbuf("l2a_vt", [128, NTC, D], BF16, es)
        FF = ctx.sbuf("l2a_ff", [128, DC, T], BF16, es)
        KF2 = ctx.sbuf("l2a_kf2", [128, 2, T], BF16, es)
        UP = XH
        AT = XH[:, :, :].rearrange("p c t -> p (c t)").rearrange("p (tc f) -> p tc f", f=D)
        ATF = VTok[:, :, :].rearrange("p tc f -> p (tc f)").rearrange("p (c t) -> p c t", t=T)
        HB = VTok[:, :, :].rearrange("p tc f -> p (tc f)").rearrange("p (m n) -> p m n", n=256)
        WCV = ctx.sbuf("l2a_wcv", [64, FD, 64], BF16, es)
        FAT = ctx.sbuf("l2a_fat", [128, NTC, NH, FD], BF16, es)
        EFUT = FAT
        DN = ctx.sbuf("l2a_dn", [128, NTC, NH], F32, es)
        QS = ctx.sbuf("l2a_qs", [64, T], BF16, es)
        FS = ctx.sbuf("l2a_fs", [64, T], BF16, es)
        LT = {}
        for nm in ["L2a_ln1g", "L2a_ln1b", "L2a_ln2g", "L2a_ln2b"]:
            LT[nm] = load_row(ctx, nm, es)
        load_row(ctx, "c_tril", es)
        load_row(ctx, "c_onesq", es)
        hpair = load_row(ctx, "c_hpair", es)
        scr = ctx.sb["scr"]
        ident = ctx.sb["c_ident"]
        xsp = nc.dram_tensor("l2a_xspill", [D, T], F32).ap()
        zsp = nc.dram_tensor("l2a_zspill", [D, T], F32).ap()
        fsp = nc.dram_tensor("l2a_fspill", [D, T], BF16).ap()
        ctx.inc(G.dma_start(WCV[:, :, :], ctx.dram["L2a_wconv"]), "ldw", 16)

        ctx.inc(nc.sync.dma_start(xsp.rearrange("(c p) t -> p c t", p=128), X[:, :, :]), "st", 16)
        nc.sync.wait_ge(ctx.sem["st"], ctx.val["st"])
        ctx.barrier()
        with contextlib.ExitStack() as esA:
            ctx.psum("ln_ps", [128, 2, TB], F32, scope=esA)
            for o in range(0, T, TB):
                emit_ln(ctx, X, XH, o, o + TB, out_f32=X, f32_t0=o)
            ctx.wait(PE, "ev")
            emit_ln(ctx, X, XH, 0, T, gamma=LT["L2a_ln1g"], beta=LT["L2a_ln1b"])
        ctx.wait(nc.sync, "ev")
        ctx.inc(nc.sync.dma_start(zsp.rearrange("(c p) t -> p c t", p=128), X[:, :, :]), "st", 16)
        nc.sync.wait_ge(ctx.sem["st"], ctx.val["st"])
        if os.environ.get("L2A_DBG"):
            dz = nc.dram_tensor("dbg_z", [D, T], F32, kind="ExternalOutput").ap()
            dxh = nc.dram_tensor("dbg_xh", [D, T], BF16, kind="ExternalOutput").ap()
            ctx.inc(nc.sync.dma_start(dz.rearrange("(c p) t -> p c t", p=128), X[:, :, :]), "st", 16)
            ctx.inc(nc.sync.dma_start(dxh.rearrange("(c p) t -> p c t", p=128), XH[:, :, :]), "st", 16)
            nc.sync.wait_ge(ctx.sem["st"], ctx.val["st"])
        ctx.barrier()

        with contextlib.ExitStack() as esB:
            ctx.psum("proj_ps", [128, 6, 512], F32, scope=esB)
            ctx.sbuf("wbuf", [128, 2, 8, 512], BF16, scope=esB)
            emit_proj(ctx, "L2a_wqkvfT", XH, 8, T, [128] * 8, dst=QF,
                      out_scale=0.125, wait_src=("ev",))
            emit_proj(ctx, "L2a_wkT", XH, 8, T, [128] * 8, dst=KF)
            emit_proj_tok(ctx, "L2a_wvT", XH, 8, D, VTok)
            emit_proj(ctx, "L2a_wfT", XH, 8, T, [128] * 8, dst=FF)
        if os.environ.get("L2A_DBG"):
            for nm, apx in [("dbg_qf", QF), ("dbg_kf", KF), ("dbg_ff", FF[:, :, :])]:
                d = nc.dram_tensor(nm, [D, T], BF16, kind="ExternalOutput").ap()
                nc.sync.wait_ge(ctx.sem["ev"], ctx.val["ev"])
                ctx.inc(nc.sync.dma_start(d.rearrange("(c p) t -> p c t", p=128), apx), "st", 16)
            dv = nc.dram_tensor("dbg_vt", [T, D], BF16, kind="ExternalOutput").ap()
            ctx.inc(nc.sync.dma_start(dv.rearrange("(tc p) f -> p tc f", p=128), VTok[:, :, :]), "st", 16)
            nc.sync.wait_ge(ctx.sem["st"], ctx.val["st"])
            ctx.barrier()
        ctx.barrier()

        with contextlib.ExitStack() as esC:
            up_ps = ctx.psum("l2a_upps", [64, 2, 512], F32, scope=esC)
            fa_ps = ctx.psum("l2a_faps", [128, 512], F32, scope=esC)
            ctx.psum("proj_ps", [128, 4, 512], F32, scope=esC)
            ctx.sbuf("wbuf", [128, 2, 8, 512], BF16, scope=esC)
            up_ev = [ctx.val["ev"]] * 2
            for fi in range(FD):
                for h in range(NH):
                    if h % 2 == 1:
                        ctx.wait(V, "mm")
                        ctx.inc(V.tensor_copy(FS[:, :], FF[64:128, h // 2, :]), "ev")
                        fh = FS[:, :]
                        ctx.wait(PE, "ev")
                    else:
                        fh = FF[0:64, h // 2, :]
                    for n in range(4):
                        sl = n % 2
                        PE.wait_ge(ctx.sem["ev"], up_ev[sl])
                        ctx.inc(PE.matmul(up_ps[0:64, sl, :], WCV[0:64, fi, :],
                                          fh[:, 512 * n:512 * (n + 1)], start=True, stop=True), "mm")
                        V.wait_ge(ctx.sem["mm"], ctx.val["mm"])
                        ctx.inc(V.tensor_copy(UP[(h % 2) * 64:(h % 2) * 64 + 64, h // 2, 512 * n:512 * (n + 1)],
                                              up_ps[0:64, sl, :]), "ev")
                        up_ev[sl] = ctx.val["ev"]

                def kfut_evac(m, n, nsz, bank, mo, fi=fi):
                    psl = ctx.sb["proj_ps"][:, bank, 0:nsz]
                    V.wait_ge(ctx.sem["mm"], ctx.val["mm"])
                    V.wait_ge(ctx.sem["ev"], ctx.val["ev"])
                    ctx.inc(V.tensor_copy(KF2[:, m % 2, 512 * n:512 * n + nsz], psl), "ev")
                    ctx.bank_ev[bank] = ctx.val["ev"]
                    if n == 3:
                        pbuf = scr[:, 0:2048].bitcast(BF16)[:, 0:2048]
                        ctx.wait(V, "mm")
                        ctx.wait(V, "ev")
                        ctx.inc(V.tensor_tensor(pbuf, QF[:, m, :], KF2[:, m % 2, :], ALU.mult), "ev")
                        ctx.wait(PE, "ev")
                        for tc in range(NTC):
                            ctx.inc(PE.matmul(fa_ps[:, 2 * tc:2 * tc + 2],
                                              pbuf.rearrange("p (a b) -> p a b", b=128)[:, tc, :],
                                              hpair[0:128, :], start=True, stop=True), "mm")
                        V.wait_ge(ctx.sem["mm"], ctx.val["mm"])
                        ctx.inc(V.tensor_scalar(
                            FAT[:, :, 2 * m:2 * m + 2, fi],
                            fa_ps[:, 0:32].rearrange("p (tc h) -> p tc h", h=2),
                            8.0, None, ALU.mult), "ev")
                emit_proj(ctx, "L2a_wkT", UP, 8, T, [128] * 8, evac=kfut_evac,
                          wait_src=("ev",))
        ctx.barrier()

        if os.environ.get("L2A_DBG"):
            dfa = nc.dram_tensor("dbg_fat", [128, NTC, NH, FD], BF16, kind="ExternalOutput").ap()
            nc.sync.wait_ge(ctx.sem["ev"], ctx.val["ev"])
            ctx.inc(nc.sync.dma_start(dfa, FAT[:, :, :, :]), "st", 16)
            nc.sync.wait_ge(ctx.sem["st"], ctx.val["st"])
            ctx.barrier()
        with contextlib.ExitStack() as esD:
            s_ps = ctx.psum("l2a_sps", [128, 4, 512], F32, scope=esD)
            t_ps = ctx.psum("l2a_tps", [128, 2, 1024], BF16, scope=esD)
            o_ps = ctx.psum("l2a_ops", [128, 2, 512], F32, scope=esD)
            tril = ctx.sb["c_tril"]
            ops_ev = [ctx.val["ev"]] * 2
            EB = ctx.sbuf("l2a_eb", [128, 2048], BF16, esD)
            EBT = ctx.sbuf("l2a_ebt", [128, 2, 128], BF16, esD)
            for h in range(NH):
                hc, hr = h // 2, (h % 2) * 64
                if h % 2 == 1:
                    ctx.wait(V, "mm")
                    ctx.inc(V.tensor_copy(QS[:, :], QF[64:128, hc, :]), "ev")
                    ctx.wait(V, "ev")
                    ctx.inc(V.tensor_copy(FS[:, :], KF[64:128, hc, :]), "ev")
                    qh, kh = QS[:, :], FS[:, :]
                else:
                    qh, kh = QF[0:64, hc, :], KF[0:64, hc, :]
                for qc in range(NTC):
                    klen = 128 * (qc + 1)
                    nk = (klen + 511) // 512
                    ctx.wait(PE, "ev")
                    ctx.wait(PE, "sc")
                    for n in range(nk):
                        ksz = min(512, klen - 512 * n)
                        ctx.inc(PE.matmul(s_ps[:, n, 0:ksz], qh[:, 128 * qc:128 * (qc + 1)],
                                          kh[:, 512 * n:512 * n + ksz], start=True, stop=True), "mm")
                    V.wait_ge(ctx.sem["mm"], ctx.val["mm"])
                    rm = scr[:, 0:1]
                    rm2 = scr[:, 1:2]
                    ctx.wait(V, "sc")
                    ctx.inc(V.tensor_reduce(rm, s_ps[:, 0:nk, :].rearrange("p a b -> p (a b)")[:, 0:klen],
                                            axis=AX.X, op=ALU.max), "ev")
                    ctx.wait(V, "ev")
                    ctx.inc(V.tensor_reduce(rm2, FAT[:, qc, h, :], axis=AX.X, op=ALU.max), "ev")
                    ctx.wait(V, "ev")
                    ctx.inc(V.tensor_tensor(rm, rm, rm2, ALU.max), "ev")
                    ctx.wait(V, "ev")
                    ctx.inc(V.tensor_scalar(rm, rm, -1.0, None, ALU.mult), "ev")
                    SC.wait_ge(ctx.sem["ev"], ctx.val["ev"])
                    ctx.inc(SC.activation(EB[:, 0:klen],
                                          s_ps[:, 0:nk, :].rearrange("p a b -> p (a b)")[:, 0:klen],
                                          AF.Exp, bias=rm), "sc")
                    ctx.inc(SC.activation(EFUT[:, qc, h, :], FAT[:, qc, h, :], AF.Exp, bias=rm), "sc")
                    ctx.wait(V, "sc")
                    dg = slice(128 * qc, 128 * (qc + 1))
                    ctx.inc(V.tensor_tensor(EB[:, dg], EB[:, dg], tril[0:128, :], ALU.mult), "ev")
                    ctx.wait(V, "ev")
                    ctx.inc(V.tensor_reduce(rm2, EFUT[:, qc, h, :], axis=AX.X, op=ALU.add), "ev")
                    ctx.wait(PE, "ev")
                    ctx.wait(PE, "sc")
                    onesq = ctx.sb["c_onesq"]
                    for kc in range(qc + 1):
                        sl2 = kc % 2
                        PE.wait_ge(ctx.sem["ev"], ctx.tps_ev[sl2])
                        ctx.inc(PE.transpose(t_ps[:, sl2, 0:128], EB[:, 128 * kc:128 * (kc + 1)],
                                             ident[:, :]), "mm")
                        V.wait_ge(ctx.sem["mm"], ctx.val["mm"])
                        ctx.inc(V.tensor_copy(EBT[:, sl2, :], t_ps[:, sl2, 0:128]), "ev")
                        ctx.tps_ev[sl2] = ctx.val["ev"]
                        osl = qc % 2
                        if kc == 0:
                            PE.wait_ge(ctx.sem["ev"], ops_ev[osl])
                        PE.wait_ge(ctx.sem["ev"], ctx.val["ev"])
                        PE.matmul(s_ps[:, 0, 0:1], EBT[:, sl2, :], onesq[0:128, 0:1],
                                  start=(kc == 0), stop=(kc == qc))
                        ctx.inc(PE.matmul(o_ps[:, osl, 0:64], EBT[:, sl2, :],
                                          VTok[:, kc, 64 * h:64 * (h + 1)],
                                          start=(kc == 0), stop=(kc == qc)), "mm")
                    V.wait_ge(ctx.sem["mm"], ctx.val["mm"])
                    osl = qc % 2
                    ctx.inc(V.tensor_tensor(DN[:, qc, h:h + 1], s_ps[:, 0, 0:1], rm2,
                                            ALU.add), "ev")
                    ctx.inc(V.tensor_copy(AT[:, qc, 64 * h:64 * (h + 1)], o_ps[:, osl, 0:64]), "ev")
                    ops_ev[osl] = ctx.val["ev"]
        ctx.barrier()

        with contextlib.ExitStack() as esE:
            up_ps = ctx.psum("l2a_upps2", [64, 2, 512], F32, scope=esE)
            vf_ps = ctx.psum("l2a_vfps", [128, 2, 512], F32, scope=esE)
            ctx.sbuf("wbuf", [128, 2, 8, 512], BF16, scope=esE)
            up_ev = [ctx.val["ev"]] * 2
            vf_ev = [ctx.val["ev"]] * 2
            for fi in range(FD):
                for h in range(NH):
                    if h % 2 == 1:
                        ctx.wait(V, "mm")
                        ctx.inc(V.tensor_copy(FS[:, :], FF[64:128, h // 2, :]), "ev")
                        fh = FS[:, :]
                        ctx.wait(PE, "ev")
                    else:
                        fh = FF[0:64, h // 2, :]
                    for n in range(4):
                        sl = n % 2
                        PE.wait_ge(ctx.sem["ev"], up_ev[sl])
                        ctx.inc(PE.matmul(up_ps[0:64, sl, :], WCV[0:64, fi, :],
                                          fh[:, 512 * n:512 * (n + 1)], start=True, stop=True), "mm")
                        V.wait_ge(ctx.sem["mm"], ctx.val["mm"])
                        ctx.inc(V.tensor_copy(UP2[(h % 2) * 64:(h % 2) * 64 + 64, h // 2, 512 * n:512 * (n + 1)],
                                              up_ps[0:64, sl, :]), "ev")
                        up_ev[sl] = ctx.val["ev"]
                wap = ctx.dram["L2a_wvT"]
                wbuf = ctx.sb["wbuf"]
                for j in range(2):
                    ctx.wait(G, "mm")
                    ctx.inc(G.dma_start(wbuf[:, j, 0:8, :],
                                        wap[:, 512 * j:512 * (j + 1)].rearrange("(c p) m -> p c m", p=128)),
                            "ldwA" if j == 0 else "ldwB", 16)
                ctx.wait(PE, "ev")
                for j in range(2):
                    ctx.wait(PE, "ldwA" if j == 0 else "ldwB")
                    for tc in range(NTC):
                        sl = tc % 2
                        PE.wait_ge(ctx.sem["ev"], vf_ev[sl])
                        for c in range(DC):
                            inst = PE.matmul(vf_ps[:, sl, :], UP2[:, c, 128 * tc:128 * (tc + 1)],
                                             wbuf[:, j, c, :], start=(c == 0), stop=(c == DC - 1))
                        ctx.inc(inst, "mm")
                        V.wait_ge(ctx.sem["mm"], ctx.val["mm"])
                        efb = EFUT[:, tc, 8 * j:8 * j + 8, fi].unsqueeze(2).broadcast_to([128, 8, 64])
                        tb1 = scr[:, 0:512].rearrange("p (h d) -> p h d", d=64)
                        ctx.wait(V, "pl")
                        ctx.inc(V.tensor_tensor(tb1, vf_ps[:, sl, :].rearrange("p (h d) -> p h d", d=64),
                                                efb, ALU.mult), "ev")
                        vf_ev[sl] = ctx.val["ev"]
                        ctx.wait(V, "ev")
                        atsl = AT[:, tc, 512 * j:512 * (j + 1)]
                        ctx.inc(V.tensor_tensor(atsl, atsl, scr[:, 0:512], ALU.add), "pl")
        ctx.barrier()

        if os.environ.get("L2A_DBG"):
            dat = nc.dram_tensor("dbg_at", [T, D], BF16, kind="ExternalOutput").ap()
            ddn = nc.dram_tensor("dbg_dn", [128, NTC, NH], F32, kind="ExternalOutput").ap()
            nc.sync.wait_ge(ctx.sem["pl"], ctx.val["pl"])
            nc.sync.wait_ge(ctx.sem["ev"], ctx.val["ev"])
            ctx.inc(nc.sync.dma_start(dat.rearrange("(tc p) f -> p tc f", p=128), AT[:, :, :]), "st", 16)
            ctx.inc(nc.sync.dma_start(ddn, DN[:, :, :]), "st", 16)
            nc.sync.wait_ge(ctx.sem["st"], ctx.val["st"])
            ctx.barrier()
        for tc in range(NTC):
            idn = scr[:, 0:16]
            ctx.wait(V, "pl")
            ctx.wait(V, "ev")
            ctx.inc(V.reciprocal(idn, DN[:, tc, :]), "ev")
            ctx.wait(V, "ev")
            idb = idn.unsqueeze(2).broadcast_to([128, 16, 64])
            atv = AT[:, tc, :].rearrange("p (h d) -> p h d", d=64)
            ctx.inc(V.tensor_tensor(atv, atv, idb, ALU.mult), "ev")
        with contextlib.ExitStack() as esT:
            t_ps = ctx.psum("l2a_tp2", [128, 2, 1024], BF16, scope=esT)
            ctx.wait(PE, "ev")
            for tc in range(NTC):
                for fc in range(DC):
                    sl = (tc * DC + fc) % 2
                    PE.wait_ge(ctx.sem["ev"], ctx.tps_ev[sl])
                    ctx.inc(PE.transpose(t_ps[:, sl, 0:128], AT[:, tc, 128 * fc:128 * (fc + 1)],
                                         ident[:, :]), "mm")
                    V.wait_ge(ctx.sem["mm"], ctx.val["mm"])
                    ctx.inc(V.tensor_copy(ATF[:, fc, 128 * tc:128 * (tc + 1)], t_ps[:, sl, 0:128]), "ev")
                    ctx.tps_ev[sl] = ctx.val["ev"]
        ctx.barrier()
        ctx.inc(nc.sync.dma_start(X[:, :, :], zsp.rearrange("(c p) t -> p c t", p=128)), "ldx", 16)
        nc.sync.wait_ge(ctx.sem["ldx"], ctx.val["ldx"])
        ctx.barrier()
        with contextlib.ExitStack() as esP:
            ctx.psum("proj_ps", [128, 6, 512], F32, scope=esP)
            ctx.sbuf("wbuf", [128, 2, 8, 512], BF16, scope=esP)
            emit_proj(ctx, "L2a_wprojT", ATF, 8, T, [128] * 8,
                      resid=X, resid_scale=1.0, wait_src=("ev",))
        ctx.barrier()

        with contextlib.ExitStack() as esG:
            ctx.psum("ln_ps", [128, 2, TB], F32, scope=esG)
            ctx.psum("proj_ps", [128, 6, 512], F32, scope=esG)
            ctx.sbuf("wbuf", [128, 2, 32, 128], BF16, scope=esG)
            emit_ln(ctx, X, XH, 0, T, gamma=LT["L2a_ln2g"], beta=LT["L2a_ln2b"])
            for hb in range(8):
                o = 256 * hb
                emit_proj(ctx, "L2a_wff1", XH, 8, 256, [128] * 32, act=AF.Gelu,
                          dst=HB, src_t0=o, tile_n=256, wait_src=("ev",))
                emit_proj(ctx, "L2a_wff2", HB, 32, 256, [128] * 8,
                          resid=X, resid_scale=1.0, dst_t0=o, tile_n=256, wait_src=("ev",))
        ctx.barrier()
        ctx.inc(G.dma_start(X[:, :, :], xsp.rearrange("(c p) t -> p c t", p=128),
                            accum_op=ALU.add), "ldw", 16)
        G.wait_ge(ctx.sem["ldw"], ctx.val["ldw"])
        ctx.barrier()



def emit_l3_attn(ctx):
    nc = ctx.nc
    V, SC, PE, G = nc.vector, nc.scalar, nc.tensor, nc.gpsimd
    X, XH = ctx.sb["X"], ctx.sb["XH"]
    NH, HD, NB, CR = 8, 64, 64, 32
    NTC = T // 128
    topk_on_v = bool(int(os.environ.get("L3A_TOPK_V", "0")))
    es = contextlib.ExitStack()
    with es:
        pool = X[:, :, :].rearrange("p c t -> p (c t)").bitcast(BF16)
        QTok = pool[:, 0:8192].rearrange("p (tc f) -> p tc f", f=512)
        MIX = pool[:, 8192:16384].rearrange("p (tc f) -> p tc f", f=512)
        MIXF = pool[:, 16384:24576].rearrange("p (c t) -> p c t", t=T)
        Y1F = XH
        QFM = ctx.sbuf("l3a_qfm", [128, 4, T], BF16, es)
        KT = ctx.sbuf("l3a_kt", [128, NTC, HD], BF16, es)
        VT = ctx.sbuf("l3a_vt", [128, NTC, HD], BF16, es)
        KFM = ctx.sbuf("l3a_kfm", [64, T], BF16, es)
        KCF = ctx.sbuf("l3a_kcf", [64, NB], BF16, es)
        KCT = ctx.sbuf("l3a_kct", [64, HD], BF16, es)
        VCT = ctx.sbuf("l3a_vct", [64, HD], BF16, es)
        QS2 = ctx.sbuf("l3a_qs2", [64, 128], BF16, es)
        EC = ctx.sbuf("l3a_ec", [128, NH, NB], BF16, es)
        SELM = ctx.sbuf("l3a_selm", [128, NH, NB], BF16, es)
        ECT = ctx.sbuf("l3a_ect", [64, NH, 128], BF16, es)
        G3 = ctx.sbuf("l3a_g3", [128, NH, 3], F32, es)
        SINKB = ctx.sbuf("l3a_sinkb", [128, NH], F32, es)
        EB = ctx.sbuf("l3a_eb", [128, T], BF16, es)
        EBT = ctx.sbuf("l3a_ebt", [128, 2, 128], BF16, es)
        GQB = ctx.sbuf("l3a_gqb", [128, HD], F32, es)
        GKB = ctx.sbuf("l3a_gkb", [128, HD], F32, es)
        SCM2 = ctx.sbuf("l3a_scm2", [128, NH, NB], F32, es)
        scr = ctx.sb["scr"]
        ident = ctx.sb["c_ident"]
        xsp = nc.dram_tensor("l3a_xspill", [D, T], F32).ap()
        for rn in ["L3a_gqr", "L3a_gkr", "L3a_bgr", "L3a_skr", "L3a_wg",
                   "c_l3cos", "c_l3sin", "c_l3cmsafe", "c_l3cmsel",
                   "c_l3wmask", "c_a32b", "c_tril", "c_anyblk"]:
            load_row(ctx, rn, es)

        with contextlib.ExitStack() as esA:
            ctx.psum("ln_ps", [128, 2, TB], F32, scope=esA)
            ctx.psum("proj_ps", [128, 6, 512], F32, scope=esA)
            ctx.sbuf("wbuf", [128, 2, 8, 512], BF16, scope=esA)
            QC = ctx.sbuf("l3a_qc", [128, 2, T], BF16, esA)
            emit_ln(ctx, X, XH, 0, T)
            ctx.inc(nc.sync.dma_start(xsp.rearrange("(c p) t -> p c t", p=128), X[:, :, :]), "st", 16)
            nc.sync.wait_ge(ctx.sem["st"], ctx.val["st"])
            ctx.barrier()
            emit_proj(ctx, "L3a_wqc", XH, 8, T, [128] * 2, dst=QC, wait_src=("ev",))
            emit_proj_tok(ctx, "L3a_wqup", QC, 2, 512, QTok, wait_src=("ev",))
            emit_proj_tok(ctx, "L3a_wk", XH, 8, HD, KT)
            emit_proj_tok(ctx, "L3a_wv", XH, 8, HD, VT)
            pps = ctx.sb["proj_ps"]
            ctx.wait(PE, "ldw")
            ctx.wait(PE, "ev")
            ctx.inc(PE.matmul(pps[:, 0, 0:HD], ctx.sb["c_ones1f"][0:1, :],
                              ctx.sb["L3a_gqr"][0:1, :], start=True, stop=True), "mm")
            ctx.inc(PE.matmul(pps[:, 1, 0:HD], ctx.sb["c_ones1f"][0:1, :],
                              ctx.sb["L3a_gkr"][0:1, :], start=True, stop=True), "mm")
            ctx.inc(PE.matmul(pps[:, 2, 0:NH], ctx.sb["c_ones1f"][0:1, :],
                              ctx.sb["L3a_skr"][0:1, :], start=True, stop=True), "mm")
            ctx.wait(V, "mm")
            ctx.inc(V.tensor_copy(GQB[:, :], pps[:, 0, 0:HD]), "ev")
            ctx.inc(V.tensor_copy(GKB[:, :], pps[:, 1, 0:HD]), "ev")
            ctx.inc(V.tensor_copy(SINKB[:, :], pps[:, 2, 0:NH]), "ev")
            cosb = ctx.sb["c_l3cos"]
            sinb = ctx.sb["c_l3sin"]
            ctx.wait(V, "ldw")
            for tc in range(NTC):
                for QK, nh, gb in ((QTok, NH, GQB), (KT, 1, GKB)):
                    wdt = HD * nh
                    hv = QK[:, tc, 0:wdt].rearrange("p (h d) -> p h d", d=HD)
                    x1 = hv[:, :, 0:16]
                    x2 = hv[:, :, 16:32]
                    cb = cosb[:, tc, :].unsqueeze(1).broadcast_to([128, nh, 16])
                    sb_ = sinb[:, tc, :].unsqueeze(1).broadcast_to([128, nh, 16])
                    t1 = scr[:, 0:128].rearrange("p (h j) -> p h j", j=16)[:, 0:nh, :]
                    t2 = scr[:, 128:256].rearrange("p (h j) -> p h j", j=16)[:, 0:nh, :]
                    t3 = scr[:, 256:384].rearrange("p (h j) -> p h j", j=16)[:, 0:nh, :]
                    t4 = scr[:, 384:512].rearrange("p (h j) -> p h j", j=16)[:, 0:nh, :]
                    ctx.wait(V, "ev")
                    ctx.wait(V, "sc")
                    ctx.inc(V.tensor_tensor(t1, x1, cb, ALU.mult), "ev")
                    ctx.inc(V.tensor_tensor(t2, x2, sb_, ALU.mult), "ev")
                    ctx.inc(V.tensor_tensor(t3, x1, sb_, ALU.mult), "ev")
                    ctx.inc(V.tensor_tensor(t4, x2, cb, ALU.mult), "ev")
                    ctx.wait(V, "ev")
                    ctx.inc(V.tensor_tensor(x1, t1, t2, ALU.subtract), "ev")
                    ctx.inc(V.tensor_tensor(x2, t3, t4, ALU.add), "ev")
                    ctx.wait(SC, "ev")
                    sq = scr[:, 512:512 + wdt]
                    ctx.inc(SC.activation(sq, QK[:, tc, 0:wdt], AF.Square), "sc")
                    ctx.wait(V, "sc")
                    ssum = scr[:, 1024:1024 + nh]
                    ctx.inc(V.tensor_reduce(ssum.rearrange("p (h o) -> p h o", o=1),
                                            sq.rearrange("p (h d) -> p h d", d=HD),
                                            axis=AX.X, op=ALU.add), "ev")
                    ctx.wait(SC, "ev")
                    ctx.inc(SC.activation(ssum, ssum, AF.Ln, scale=np.float32(1.0 / HD),
                                          bias=np.float32(1e-6)), "sc")
                    ctx.wait(SC, "sc")
                    ctx.inc(SC.activation(ssum, ssum, AF.Exp, scale=np.float32(-0.5)), "sc")
                    ctx.wait(V, "sc")
                    rse = ssum.rearrange("p (h o) -> p h o", o=1).broadcast_to([128, nh, HD])
                    ctx.inc(V.tensor_tensor(hv, hv, rse, ALU.mult), "ev")
                    ctx.wait(V, "ev")
                    ctx.inc(V.tensor_tensor(hv, hv, gb[:, :].unsqueeze(1).broadcast_to([128, nh, HD]),
                                            ALU.mult), "ev")
        with contextlib.ExitStack() as esT:
            t_ps = ctx.psum("l3a_tp", [128, 2, 1024], BF16, scope=esT)
            mps = ctx.psum("l3a_mmps", [128, 2, 512], F32, scope=esT)
            ctx.wait(PE, "ev")
            for tc in range(NTC):
                for fc in range(4):
                    sl = (tc * 4 + fc) % 2
                    PE.wait_ge(ctx.sem["ev"], ctx.tps_ev[sl])
                    ctx.inc(PE.transpose(t_ps[:, sl, 0:128], QTok[:, tc, 128 * fc:128 * (fc + 1)],
                                         ident[:, :]), "mm")
                    V.wait_ge(ctx.sem["mm"], ctx.val["mm"])
                    ctx.inc(V.tensor_copy(QFM[:, fc, 128 * tc:128 * (tc + 1)], t_ps[:, sl, 0:128]), "ev")
                    ctx.tps_ev[sl] = ctx.val["ev"]
                sl = tc % 2
                PE.wait_ge(ctx.sem["ev"], ctx.tps_ev[sl])
                ctx.inc(PE.transpose(t_ps[0:HD, sl, 0:128], KT[:, tc, :], ident[:, :]), "mm")
                V.wait_ge(ctx.sem["mm"], ctx.val["mm"])
                ctx.inc(V.tensor_copy(KFM[:, 128 * tc:128 * (tc + 1)], t_ps[0:HD, sl, 0:128]), "ev")
                ctx.tps_ev[sl] = ctx.val["ev"]
            a32 = ctx.sb["c_a32b"]
            ctx.wait(PE, "ldw")
            for tc in range(NTC):
                ctx.inc(PE.matmul(mps[0:NB, 0, 0:HD], a32[:, tc, :], KT[:, tc, :],
                                  start=(tc == 0), stop=(tc == NTC - 1)), "mm")
            for tc in range(NTC):
                ctx.inc(PE.matmul(mps[0:NB, 1, 0:HD], a32[:, tc, :], VT[:, tc, :],
                                  start=(tc == 0), stop=(tc == NTC - 1)), "mm")
            ctx.wait(V, "mm")
            ctx.inc(V.tensor_copy(KCT[:, :], mps[0:NB, 0, 0:HD]), "ev")
            ctx.inc(V.tensor_copy(VCT[:, :], mps[0:NB, 1, 0:HD]), "ev")
            sl = 0
            PE.wait_ge(ctx.sem["ev"], ctx.tps_ev[sl])
            ctx.wait(PE, "ev")
            ctx.inc(PE.transpose(t_ps[0:HD, sl, 0:NB], KCT[:, :], ident[0:NB, 0:NB]), "mm")
            ctx.wait(V, "mm")
            ctx.inc(V.tensor_copy(KCF[:, :], t_ps[0:HD, sl, 0:NB]), "ev")
            ctx.tps_ev[sl] = ctx.val["ev"]
        ctx.barrier()

        with contextlib.ExitStack() as esB:
            s_ps = ctx.psum("l3a_sps", [128, 4, 512], F32, scope=esB)
            t_ps = ctx.psum("l3a_tp2", [128, 2, 1024], BF16, scope=esB)
            m_ps = ctx.psum("l3a_mps", [128, 2, 512], F32, scope=esB)
            wg = ctx.sb["L3a_wg"]
            wmask = ctx.sb["c_l3wmask"]
            tril = ctx.sb["c_tril"]
            cmsafe = ctx.sb["c_l3cmsafe"]
            cmsel = ctx.sb["c_l3cmsel"]
            TG = V
            tgn = "ev"

            def qhead(h, qc):
                hc = h // 2
                if h % 2 == 1:
                    ctx.wait(V, "mm")
                    ctx.inc(V.tensor_copy(QS2[:, :], QFM[64:128, hc, 128 * qc:128 * (qc + 1)]), "ev")
                    ctx.wait(PE, "ev")
                    return QS2[:, :]
                return QFM[0:64, hc, 128 * qc:128 * (qc + 1)]

            for qc in range(NTC):
                klen = 128 * (qc + 1)
                nk = (klen + 511) // 512
                ctx.wait(PE, "ev")
                ctx.wait(PE, "sc")
                for h in range(NH):
                    qop = qhead(h, qc)
                    PE.matmul(m_ps[:, 0, 8 * h:8 * h + 8], qop, wg[:, :], start=True, stop=False)
                    ctx.inc(PE.matmul(m_ps[:, 0, 8 * h:8 * h + 8], ctx.sb["c_ones1"][0:1, 0:128],
                                      ctx.sb["L3a_bgr"][0:1, :], start=False, stop=True), "mm")
                SC.wait_ge(ctx.sem["mm"], ctx.val["mm"])
                SC.wait_ge(ctx.sem["ev"], ctx.val["ev"])
                ctx.inc(SC.activation(G3[:, :, :],
                                      m_ps[:, 0, 0:64].rearrange("p (h g) -> p h g", g=8)[:, :, 0:3],
                                      AF.Sigmoid), "sc")
                ctx.wait(V, "sc")
                gsum = scr[:, 0:NH]
                ctx.inc(V.tensor_reduce(gsum.rearrange("p (h o) -> p h o", o=1), G3[:, :, :],
                                        axis=AX.X, op=ALU.add), "ev")
                ctx.wait(V, "ev")
                ctx.inc(V.tensor_scalar(gsum, gsum, 1e-6, None, ALU.max), "ev")
                ctx.wait(V, "ev")
                ctx.inc(V.reciprocal(gsum, gsum), "ev")
                ctx.wait(V, "ev")
                ctx.inc(V.tensor_tensor(G3[:, :, :], G3[:, :, :],
                                        gsum.rearrange("p (h o) -> p h o", o=1).broadcast_to([128, NH, 3]),
                                        ALU.mult), "ev")
                ctx.wait(PE, "ev")
                ctx.wait(PE, "sc")
                for h in range(NH):
                    qop = qhead(h, qc)
                    ctx.inc(PE.matmul(m_ps[:, h // 4, 64 * (h % 4):64 * (h % 4) + 64],
                                      qop, KCF[:, :], start=True, stop=True), "mm")
                V.wait_ge(ctx.sem["mm"], ctx.val["mm"])
                V.wait_ge(ctx.sem["sc"], ctx.val["sc"])
                TG.wait_ge(ctx.sem[tgn], ctx.val[tgn])
                scm = scr[:, 512:1024].rearrange("p (h n) -> p h n", n=NB)
                for a in range(2):
                    sc_raw = m_ps[:, a, 0:256].rearrange("p (h n) -> p h n", n=NB)
                    cme = cmsafe[:, NB * qc:NB * (qc + 1)].unsqueeze(1).broadcast_to([128, 4, NB])
                    ctx.inc(V.tensor_tensor(scm[:, 4 * a:4 * a + 4, :], sc_raw, cme, ALU.add), "ev")
                    cme2 = cmsel[:, NB * qc:NB * (qc + 1)].unsqueeze(1).broadcast_to([128, 4, NB])
                    ctx.inc(V.tensor_tensor(SCM2[:, 4 * a:4 * a + 4, :], sc_raw, cme2, ALU.add), "ev")
                ctx.wait(V, "ev")
                rm8 = scr[:, 0:NH]
                M8n = scr[:, 8:8 + NH]
                es8 = scr[:, 16:16 + NH]
                d8 = scr[:, 24:24 + NH]
                gd0 = scr[:, 32:32 + NH]
                m16 = scr[:, 40:40 + NH]
                ctx.inc(V.tensor_reduce(rm8.rearrange("p (h o) -> p h o", o=1), scm,
                                        axis=AX.X, op=ALU.max), "ev")
                ctx.wait(V, "ev")
                ctx.inc(V.tensor_scalar(rm8, rm8, 0.125, None, ALU.mult), "ev")
                ctx.wait(V, "ev")
                ctx.inc(V.tensor_tensor(rm8, rm8, SINKB[:, :], ALU.max), "ev")
                ctx.wait(V, "ev")
                ctx.inc(V.tensor_scalar(M8n, rm8, -1.0, None, ALU.mult), "ev")
                ctx.wait(V, "ev")
                ctx.inc(V.tensor_tensor(es8, SINKB[:, :], M8n, ALU.add), "ev")
                SC.wait_ge(ctx.sem["ev"], ctx.val["ev"])
                SC.wait_ge(ctx.sem["mm"], ctx.val["mm"])
                for h in range(NH):
                    ctx.inc(SC.activation(EC[:, h, :], scm[:, h, :], AF.Exp,
                                          bias=M8n[:, h:h + 1], scale=np.float32(0.125)), "sc")
                ctx.inc(SC.activation(es8, es8, AF.Exp), "sc")
                ctx.wait(V, "sc")
                ctx.inc(V.tensor_reduce(d8.rearrange("p (h o) -> p h o", o=1), EC[:, :, :],
                                        axis=AX.X, op=ALU.add), "ev")
                ctx.wait(V, "ev")
                ctx.inc(V.tensor_tensor(d8, d8, es8, ALU.add), "ev")
                ctx.wait(V, "ev")
                ctx.inc(V.reciprocal(d8, d8), "ev")
                ctx.wait(V, "ev")
                ctx.inc(V.tensor_tensor(gd0, d8, G3[:, :, 0], ALU.mult), "ev")
                if qc == 0:
                    ctx.wait(V, "ev")
                    ctx.inc(V.tensor_tensor(gd0, gd0,
                                            ctx.sb["c_anyblk"][:, 0:1].broadcast_to([128, NH]),
                                            ALU.mult), "ev")
                ctx.wait(V, "ev")
                ctx.inc(V.tensor_tensor(EC[:, :, :], EC[:, :, :],
                                        gd0.rearrange("p (h o) -> p h o", o=1).broadcast_to([128, NH, NB]),
                                        ALU.mult), "ev")
                msk = scr[:, 1024:1536].rearrange("p (h n) -> p h n", n=NB)
                for it in range(16):
                    TG.wait_ge(ctx.sem["ev"], ctx.val["ev"])
                    TG.wait_ge(ctx.sem[tgn], ctx.val[tgn])
                    ctx.inc(TG.tensor_reduce(m16.rearrange("p (h o) -> p h o", o=1), SCM2[:, :, :],
                                             axis=AX.X, op=ALU.max), tgn)
                    if it < 15:
                        TG.wait_ge(ctx.sem[tgn], ctx.val[tgn])
                        ctx.inc(TG.tensor_tensor(msk, SCM2[:, :, :],
                                                 m16.rearrange("p (h o) -> p h o", o=1).broadcast_to([128, NH, NB]),
                                                 ALU.is_ge), tgn)
                        TG.wait_ge(ctx.sem[tgn], ctx.val[tgn])
                        ctx.inc(TG.scalar_tensor_tensor(SCM2[:, :, :], msk, -1e30, SCM2[:, :, :],
                                                        ALU.mult, ALU.add), tgn)
                V.wait_ge(ctx.sem[tgn], ctx.val[tgn])
                for a in range(2):
                    sc_raw = m_ps[:, a, 0:256].rearrange("p (h n) -> p h n", n=NB)
                    cme2 = cmsel[:, NB * qc:NB * (qc + 1)].unsqueeze(1).broadcast_to([128, 4, NB])
                    ctx.inc(V.tensor_tensor(msk[:, 4 * a:4 * a + 4, :], sc_raw, cme2, ALU.add), "ev")
                ctx.wait(V, "ev")
                ctx.inc(V.tensor_tensor(SELM[:, :, :], msk,
                                        m16.rearrange("p (h o) -> p h o", o=1).broadcast_to([128, NH, NB]),
                                        ALU.is_ge), "ev")
                ctx.wait(PE, "ev")
                for h in range(NH):
                    sl = h % 2
                    PE.wait_ge(ctx.sem["ev"], ctx.tps_ev[sl])
                    ctx.inc(PE.transpose(t_ps[0:NB, sl, 0:128], EC[:, h, :], ident[:, :]), "mm")
                    V.wait_ge(ctx.sem["mm"], ctx.val["mm"])
                    ctx.inc(V.tensor_copy(ECT[:, h, :], t_ps[0:NB, sl, 0:128]), "ev")
                    ctx.tps_ev[sl] = ctx.val["ev"]
                for h in range(NH):
                    qop = qhead(h, qc)
                    ctx.wait(PE, "sc")
                    for n in range(nk):
                        ksz = min(512, klen - 512 * n)
                        ctx.inc(PE.matmul(s_ps[:, n, 0:ksz], qop, KFM[:, 512 * n:512 * n + ksz],
                                          start=True, stop=True), "mm")
                    V.wait_ge(ctx.sem["mm"], ctx.val["mm"])
                    ctx.wait(V, "sc")
                    rm = scr[:, 0:1]
                    nrm = scr[:, 1:2]
                    sbv = scr[:, 2:3]
                    dw = scr[:, 3:4]
                    gdw = scr[:, 4:5]
                    ds = scr[:, 5:6]
                    gds = scr[:, 6:7]
                    rmk = scr[:, 8:8 + nk]
                    for n in range(nk):
                        ksz = min(512, klen - 512 * n)
                        ctx.inc(V.tensor_reduce(rmk[:, n:n + 1], s_ps[:, n, 0:ksz],
                                                axis=AX.X, op=ALU.max), "ev")
                    ctx.wait(V, "ev")
                    ctx.inc(V.tensor_reduce(rm, rmk, axis=AX.X, op=ALU.max), "ev")
                    ctx.wait(V, "ev")
                    ctx.inc(V.tensor_scalar(rm, rm, 0.125, None, ALU.mult), "ev")
                    ctx.wait(V, "ev")
                    ctx.inc(V.tensor_tensor(rm, rm, SINKB[:, h:h + 1], ALU.max), "ev")
                    ctx.wait(V, "ev")
                    ctx.inc(V.tensor_scalar(nrm, rm, -1.0, None, ALU.mult), "ev")
                    ctx.wait(V, "ev")
                    ctx.inc(V.tensor_tensor(sbv, SINKB[:, h:h + 1], nrm, ALU.add), "ev")
                    SC.wait_ge(ctx.sem["ev"], ctx.val["ev"])
                    for n in range(nk):
                        ksz = min(512, klen - 512 * n)
                        ctx.inc(SC.activation(EB[:, 512 * n:512 * n + ksz], s_ps[:, n, 0:ksz],
                                              AF.Exp, bias=nrm, scale=np.float32(0.125)), "sc")
                    ctx.inc(SC.activation(sbv, sbv, AF.Exp), "sc")
                    ctx.wait(V, "sc")
                    if qc >= 4:
                        b0, bw = 128 * qc - 512, 640
                    else:
                        b0, bw = 0, klen
                    ew = scr[:, 1024:1024 + bw].bitcast(BF16)[:, 0:bw]
                    if qc >= 4:
                        ctx.inc(V.tensor_tensor(ew, EB[:, b0:b0 + bw], wmask[:, 0:bw], ALU.mult), "ev")
                    else:
                        ctx.inc(V.tensor_copy(ew, EB[:, b0:b0 + bw]), "ev")
                        ctx.wait(V, "ev")
                        dg = slice(128 * qc - b0, 128 * qc - b0 + 128)
                        ctx.inc(V.tensor_tensor(ew[:, dg], ew[:, dg], tril[:, :], ALU.mult), "ev")
                    ctx.wait(V, "ev")
                    ctx.inc(V.tensor_reduce(dw, ew, axis=AX.X, op=ALU.add), "ev")
                    ctx.wait(V, "ev")
                    ctx.inc(V.tensor_tensor(dw, dw, sbv, ALU.add), "ev")
                    ctx.wait(V, "ev")
                    ctx.inc(V.reciprocal(dw, dw), "ev")
                    ctx.wait(V, "ev")
                    ctx.inc(V.tensor_tensor(gdw, dw, G3[:, h, 2:3], ALU.mult), "ev")
                    nbk = klen // CR
                    esel3 = EB[:, 0:klen].rearrange("p (n w) -> p n w", w=CR)
                    sme = SELM[:, h, 0:nbk].unsqueeze(2).broadcast_to([128, nbk, CR])
                    ctx.wait(V, "ev")
                    ctx.inc(V.tensor_tensor(esel3, esel3, sme, ALU.mult), "ev")
                    ctx.wait(V, "ev")
                    dgf = slice(128 * qc, 128 * qc + 128)
                    ctx.inc(V.tensor_tensor(EB[:, dgf], EB[:, dgf], tril[:, :], ALU.mult), "ev")
                    ctx.wait(V, "ev")
                    ctx.inc(V.tensor_reduce(ds, EB[:, 0:klen], axis=AX.X, op=ALU.add), "ev")
                    ctx.wait(V, "ev")
                    ctx.inc(V.reciprocal(ds, ds), "ev")
                    ctx.wait(V, "ev")
                    ctx.inc(V.tensor_tensor(gds, ds, G3[:, h, 1:2], ALU.mult), "ev")
                    ctx.wait(V, "ev")
                    ctx.inc(V.tensor_scalar(EB[:, 0:klen], EB[:, 0:klen], gds, None, ALU.mult), "ev")
                    ctx.wait(V, "ev")
                    ctx.inc(V.tensor_scalar(ew, ew, gdw, None, ALU.mult), "ev")
                    ctx.wait(V, "ev")
                    ctx.inc(V.scalar_tensor_tensor(EB[:, b0:b0 + bw], EB[:, b0:b0 + bw], 1.0, ew,
                                                   ALU.mult, ALU.add), "ev")
                    ctx.wait(PE, "ev")
                    for kc in range(qc + 1):
                        sl2 = kc % 2
                        PE.wait_ge(ctx.sem["ev"], ctx.tps_ev[sl2])
                        ctx.inc(PE.transpose(t_ps[:, sl2, 0:128], EB[:, 128 * kc:128 * (kc + 1)],
                                             ident[:, :]), "mm")
                        V.wait_ge(ctx.sem["mm"], ctx.val["mm"])
                        ctx.inc(V.tensor_copy(EBT[:, sl2, :], t_ps[:, sl2, 0:128]), "ev")
                        ctx.tps_ev[sl2] = ctx.val["ev"]
                        PE.wait_ge(ctx.sem["ev"], ctx.val["ev"])
                        PE.matmul(m_ps[:, 1, 0:HD], EBT[:, sl2, :], VT[:, kc, :],
                                  start=(kc == 0), stop=False)
                    ctx.inc(PE.matmul(m_ps[:, 1, 0:HD], ECT[:, h, :], VCT[:, :],
                                      start=False, stop=True), "mm")
                    V.wait_ge(ctx.sem["mm"], ctx.val["mm"])
                    ctx.inc(V.tensor_copy(MIX[:, qc, HD * h:HD * (h + 1)], m_ps[:, 1, 0:HD]), "ev")
        ctx.barrier()

        with contextlib.ExitStack() as esO:
            t_ps = ctx.psum("l3a_tp3", [128, 2, 1024], BF16, scope=esO)
            for tc in range(NTC):
                for fc in range(4):
                    sl = (tc * 4 + fc) % 2
                    PE.wait_ge(ctx.sem["ev"], ctx.tps_ev[sl])
                    ctx.inc(PE.transpose(t_ps[:, sl, 0:128], MIX[:, tc, 128 * fc:128 * (fc + 1)],
                                         ident[:, :]), "mm")
                    V.wait_ge(ctx.sem["mm"], ctx.val["mm"])
                    ctx.inc(V.tensor_copy(MIXF[:, fc, 128 * tc:128 * (tc + 1)], t_ps[:, sl, 0:128]), "ev")
                    ctx.tps_ev[sl] = ctx.val["ev"]
        ctx.barrier()
        ctx.inc(nc.sync.dma_start(X[:, 0:4, :], xsp.rearrange("(c p) t -> p c t", p=128)[:, 0:4, :]),
                "ldx", 16)
        nc.sync.wait_ge(ctx.sem["ldx"], ctx.val["ldx"])
        ctx.barrier()
        with contextlib.ExitStack() as esP:
            ctx.psum("proj_ps", [128, 6, 512], F32, scope=esP)
            ctx.sbuf("wbuf", [128, 2, 8, 512], BF16, scope=esP)
            emit_proj(ctx, "L3a_wog0", MIXF, 2, T, [128] * 4, dst=Y1F, wait_src=("ev",))
            emit_proj(ctx, "L3a_wog1", MIXF[:, 2:4, :], 2, T, [128] * 4,
                      dst=Y1F[:, 4:8, :], wait_src=("ev",))
            ctx.wait(nc.sync, "ev")
            ctx.inc(nc.sync.dma_start(X[:, 4:6, :], xsp.rearrange("(c p) t -> p c t", p=128)[:, 4:6, :]),
                    "ldx", 16)
            nc.sync.wait_ge(ctx.sem["ldx"], ctx.val["ldx"])
            ctx.barrier()
            emit_proj(ctx, "L3a_wout", XH, 8, T, [128] * 8, resid=X, wait_src=("ev",))
        ctx.wait(V, "ev")
        ctx.wait(SC, "ev")
        ctx.wait(PE, "ev")


def prep_l3_attn(inputs, wmap):
    wmap["L3a_wqc"] = np.asarray(inputs["L3_attn_w_qc"], np.float32)
    wmap["L3a_wqup"] = np.asarray(inputs["L3_attn_w_qup"], np.float32)
    wmap["L3a_wk"] = np.asarray(inputs["L3_attn_w_k"], np.float32)
    wmap["L3a_wv"] = np.asarray(inputs["L3_attn_w_v"], np.float32)
    wmap["L3a_wog0"] = np.asarray(inputs["L3_attn_w_out_group"], np.float32)[0]
    wmap["L3a_wog1"] = np.asarray(inputs["L3_attn_w_out_group"], np.float32)[1]
    wmap["L3a_wout"] = np.asarray(inputs["L3_attn_w_out"], np.float32)
    wmap["L3a_gqr"] = np.asarray(inputs["L3_attn_g_qnorm"], np.float32).reshape(1, 64)
    wmap["L3a_gkr"] = np.asarray(inputs["L3_attn_g_knorm"], np.float32).reshape(1, 64)
    wmap["L3a_skr"] = np.asarray(inputs["L3_attn_sink_logit"], np.float32).reshape(1, 8)
    wgp = np.zeros((64, 8), np.float32)
    wgp[:, 0:3] = np.asarray(inputs["L3_attn_w_gate"], np.float32)
    wmap["L3a_wg"] = wgp.astype(BF)
    bgp = np.zeros((1, 8), np.float32)
    bgp[0, 0:3] = np.asarray(inputs["L3_attn_b_gate"], np.float32)
    wmap["L3a_bgr"] = bgp.astype(BF)
    if "c_tril" not in wmap:
        wmap["c_tril"] = np.tril(np.ones((128, 128), np.float32)).astype(BF)
        hp = np.zeros((128, 2), np.float32)
        hp[0:64, 0] = 1.0
        hp[64:128, 1] = 1.0
        wmap["c_hpair"] = hp.astype(BF)
    if "c_l3cos" not in wmap:
        inv = 10000.0 ** (-np.arange(16, dtype=np.float64) / 16.0)
        ang = np.arange(T, dtype=np.float64)[:, None] * inv[None, :]
        wmap["c_l3cos"] = np.ascontiguousarray(
            np.cos(ang).astype(np.float32).reshape(16, 128, 16).transpose(1, 0, 2))
        wmap["c_l3sin"] = np.ascontiguousarray(
            np.sin(ang).astype(np.float32).reshape(16, 128, 16).transpose(1, 0, 2))
        tt = np.arange(T).reshape(16, 128).transpose(1, 0)
        nn = np.arange(64)
        valid = ((nn[None, None, :] + 1) * 32 - 1) <= tt[:, :, None]
        anyb = valid.any(-1, keepdims=True)
        safe = np.where(anyb, valid, True)
        wmap["c_l3cmsafe"] = np.where(safe, 0.0, -1e30).astype(np.float32).reshape(128, 1024)
        selpen = np.where(valid & anyb, -nn[None, None, :] * 1e-5,
                          -(1e30 + nn[None, None, :] * 1e26))
        wmap["c_l3cmsel"] = selpen.astype(np.float32).reshape(128, 1024)
        wm = np.zeros((128, 640), np.float32)
        for p in range(128):
            wm[p, p + 1:p + 513] = 1.0
        wmap["c_l3wmask"] = wm.astype(BF)
        a32 = np.zeros((2048, 64), np.float32)
        for t_ in range(2048):
            a32[t_, t_ // 32] = 1.0 / 32.0
        wmap["c_a32b"] = np.ascontiguousarray(
            a32.reshape(16, 128, 64).transpose(1, 0, 2)).astype(BF)
        ab = np.zeros((128, 8), np.float32)
        ab[31:, :] = 1.0
        wmap["c_anyblk"] = ab



def emit_l3_ffn(ctx):
    nc = ctx.nc
    V, SC, PE, G = nc.vector, nc.scalar, nc.tensor, nc.gpsimd
    X, XH = ctx.sb["X"], ctx.sb["XH"]
    NE, NI = 64, 512
    es = contextlib.ExitStack()
    with es:
        ROUT = X
        scr = ctx.sb["scr"]
        xsp = ctx.dscratch("l3f_xspill", [D, T], F32)
        CW64 = ctx.sbuf("l3f_cw", [128, 16, NE], F32, es)
        MSW = ctx.sbuf("l3f_msw", [128, 16, NE], F32, es)
        CWT = ctx.sbuf("l3f_cwt", [NE, T], F32, es)
        SIL = ctx.sbuf("l3f_sil", [128, 4, 512], BF16, es)
        T2B = ctx.sbuf("l3f_t2", [128, 4, 512], BF16, es)
        ACT4 = ctx.sbuf("l3f_act4", [128, 4, 512], BF16, es)
        cwtd = ctx.dscratch("l3f_cwtd", [NE, T], F32)

        with contextlib.ExitStack() as esR:
            ctx.psum("ln_ps", [128, 2, TB], F32, scope=esR)
            rps = ctx.psum("l3f_rps", [128, 2, 512], F32, scope=esR)
            RF32 = ctx.sbuf("l3f_rf32", [128, DC, TB], F32, esR)
            RWf = ctx.sbuf("l3f_rwf", [128, DC, NE], F32, esR)
            GBB = ctx.sbuf("l3f_gbb", [128, NE], F32, esR)
            gbr = load_row(ctx, "L3f_gbr", esR)
            ctx.inc(nc.sync.dma_start(RWf[:, :, :],
                                      ctx.dram["L3f_gwT"].rearrange("(c p) m -> p c m", p=128)), "ldx", 16)
            ctx.wait(PE, "ldw")
            ctx.wait(PE, "ldx")
            ctx.inc(PE.matmul(rps[:, 0, 0:NE], ctx.sb["c_ones1f"][0:1, :], gbr[0:1, :],
                              start=True, stop=True), "mm")
            ctx.wait(V, "mm")
            ctx.inc(V.tensor_copy(GBB[:, :], rps[:, 0, 0:NE]), "ev")
            ctx.wait(PE, "ev")
            for tb in range(NTB):
                o = tb * TB
                emit_ln(ctx, X, XH, o, o + TB, out_f32=RF32)
                ctx.wait(PE, "ev")
                for tcl in range(4):
                    bank = tcl % 2
                    PE.wait_ge(ctx.sem["ev"], ctx.rps_ev[bank])
                    for c in range(DC):
                        inst = PE.matmul(rps[:, bank, 0:NE], RF32[:, c, 128 * tcl:128 * (tcl + 1)],
                                         RWf[:, c, :], start=(c == 0), stop=(c == DC - 1))
                    ctx.inc(inst, "mm")
                    SC.wait_ge(ctx.sem["mm"], ctx.val["mm"])
                    SC.wait_ge(ctx.sem["ev"], ctx.val["ev"])
                    tc = 4 * tb + tcl
                    ctx.inc(SC.activation(CW64[:, tc, :], rps[:, bank, 0:NE], AF.Sigmoid), "sc")
                    ctx.wait(V, "sc")
                    ctx.inc(V.tensor_tensor(CW64[:, tc, :], CW64[:, tc, :], GBB[:, :], ALU.add), "ev")
                    ctx.rps_ev[bank] = ctx.val["ev"]
        gsb = scr[:, 0:128].rearrange("p (a b) -> p a b", b=8)
        ctx.wait(V, "ev")
        ctx.wait(V, "sc")
        ctx.inc(V.tensor_reduce(gsb, CW64[:, :, :].rearrange("p a (g w) -> p (a g) w", w=8),
                                axis=AX.X, op=ALU.max), "ev")
        gw = scr[:, 128:256].rearrange("p (a b) -> p a b", b=8)
        ctx.wait(V, "ev")
        ctx.inc(V.tensor_copy(gw, gsb), "ev")
        gm = scr[:, 256:272]
        for it in range(4):
            ctx.wait(V, "ev")
            ctx.inc(V.tensor_reduce(gm.rearrange("p (a o) -> p a o", o=1), gw, axis=AX.X, op=ALU.max), "ev")
            if it < 3:
                ctx.wait(V, "ev")
                msk = scr[:, 272:400].rearrange("p (a b) -> p a b", b=8)
                ctx.inc(V.tensor_tensor(msk, gw, gm.rearrange("p (a o) -> p a o", o=1).broadcast_to([128, 16, 8]),
                                        ALU.is_ge), "ev")
                ctx.wait(V, "ev")
                ctx.inc(V.scalar_tensor_tensor(gw, msk, -1e30, gw, ALU.mult, ALU.add), "ev")
        ctx.wait(V, "ev")
        gmask = scr[:, 272:400].rearrange("p (a b) -> p a b", b=8)
        ctx.inc(V.tensor_tensor(gmask, gsb, gm.rearrange("p (a o) -> p a o", o=1).broadcast_to([128, 16, 8]),
                                ALU.is_ge), "ev")
        ctx.wait(V, "ev")
        gme = gmask.unsqueeze(3).broadcast_to([128, 16, 8, 8])
        ctx.inc(V.tensor_tensor(MSW[:, :, :].rearrange("p a (g w) -> p a g w", w=8),
                                CW64[:, :, :].rearrange("p a (g w) -> p a g w", w=8),
                                gme, ALU.mult), "ev")
        ctx.wait(V, "ev")
        ctx.inc(V.tensor_copy(CW64[:, :, :], MSW[:, :, :]), "ev")
        t6 = scr[:, 400:416]
        for it in range(6):
            ctx.wait(V, "ev")
            ctx.inc(V.tensor_reduce(t6.rearrange("p (a o) -> p a o", o=1), MSW[:, :, :], axis=AX.X, op=ALU.max), "ev")
            if it < 5:
                ctx.wait(V, "ev")
                msk2 = scr[:, 416:1440].rearrange("p (a b) -> p a b", b=64)
                ctx.inc(V.tensor_tensor(msk2, MSW[:, :, :],
                                        t6.rearrange("p (a o) -> p a o", o=1).broadcast_to([128, 16, 64]),
                                        ALU.is_ge), "ev")
                ctx.wait(V, "ev")
                ctx.inc(V.scalar_tensor_tensor(MSW[:, :, :], msk2, -1e30, MSW[:, :, :], ALU.mult, ALU.add), "ev")
        ctx.wait(V, "ev")
        msk2 = scr[:, 416:1440].rearrange("p (a b) -> p a b", b=64)
        ctx.inc(V.tensor_tensor(msk2, CW64[:, :, :],
                                t6.rearrange("p (a o) -> p a o", o=1).broadcast_to([128, 16, 64]),
                                ALU.is_ge), "ev")
        ctx.wait(V, "ev")
        ctx.inc(V.tensor_tensor(CW64[:, :, :], CW64[:, :, :], msk2, ALU.mult), "ev")
        ctx.wait(V, "ev")
        s6 = scr[:, 1440:1456]
        ctx.inc(V.tensor_reduce(s6.rearrange("p (a o) -> p a o", o=1), CW64[:, :, :], axis=AX.X, op=ALU.add), "ev")
        ctx.wait(V, "ev")
        ctx.inc(V.tensor_scalar(s6, s6, 1e-8, None, ALU.add), "ev")
        ctx.wait(V, "ev")
        ctx.inc(V.reciprocal(s6, s6), "ev")
        ctx.wait(V, "ev")
        ctx.inc(V.tensor_tensor(CW64[:, :, :], CW64[:, :, :],
                                s6.rearrange("p (a o) -> p a o", o=1).broadcast_to([128, 16, 64]),
                                ALU.mult), "ev")

        with contextlib.ExitStack() as esW:
            tpw = ctx.psum("l3f_tpw", [128, 2, 512], F32, scope=esW)
            identf = ctx.sb["c_identf"]
            ctx.wait(PE, "ev")
            for tc in range(16):
                sl = tc % 2
                PE.wait_ge(ctx.sem["ev"], ctx.rps_ev[sl])
                ctx.inc(PE.transpose(tpw[0:NE, sl, 0:128], CW64[:, tc, :], identf[:, :]), "mm")
                V.wait_ge(ctx.sem["mm"], ctx.val["mm"])
                ctx.inc(V.tensor_copy(CWT[:, 128 * tc:128 * (tc + 1)], tpw[0:NE, sl, 0:128]), "ev")
                ctx.rps_ev[sl] = ctx.val["ev"]
        ctx.wait(nc.sync, "ev")
        ctx.inc(nc.sync.dma_start(cwtd, CWT[:, :]), "st", 16)

        with contextlib.ExitStack() as esA:
            ctx.psum("ln_ps", [128, 2, TB], F32, scope=esA)
            ctx.psum("proj_ps", [128, 6, 512], F32, scope=esA)
            ctx.sbuf("wbuf", [128, 2, 8, 512], BF16, scope=esA)
            G16 = ctx.sbuf("l3f_g16", [128, 8, TB], BF16, esA)
            ASH = ctx.sbuf("l3f_ash", [128, 8, TB], BF16, esA)
            emit_ln(ctx, X, XH, 0, T)
            ctx.inc(nc.sync.dma_start(xsp.rearrange("(c p) t -> p c t", p=128), X[:, :, :]), "st", 16)
            nc.sync.wait_ge(ctx.sem["st"], ctx.val["st"])
            ctx.barrier()
            for tb in range(NTB):
                o = tb * TB
                emit_proj(ctx, "L3f_sgT", XH, 8, TB, [128] * 8, act=AF.Silu,
                          dst=G16, src_t0=o, wait_src=("ev",))

                def u_evac(m, n, nsz, bank, mo):
                    V.wait_ge(ctx.sem["mm"], ctx.val["mm"])
                    V.wait_ge(ctx.sem["ev"], ctx.val["ev"])
                    ctx.inc(V.tensor_tensor(ASH[:, m, 0:nsz], ctx.sb["proj_ps"][:, bank, 0:nsz],
                                            G16[:, m, 0:nsz], ALU.mult), "ev")
                emit_proj(ctx, "L3f_suT", XH, 8, TB, [128] * 8, evac=u_evac, src_t0=o)

                def sh_evac(m, n, nsz, bank, mo, o=o):
                    V.wait_ge(ctx.sem["mm"], ctx.val["mm"])
                    ctx.inc(V.tensor_scalar(ROUT[:, m, o:o + nsz], ctx.sb["proj_ps"][:, bank, 0:nsz],
                                            0.5, None, ALU.mult), "ev")
                emit_proj(ctx, "L3f_sdT", ASH, 8, TB, [128] * 8, evac=sh_evac, wait_src=("ev",))

        ctx.barrier()
        with contextlib.ExitStack() as esE:
            ep8 = ctx.psum("l3f_ep", [128, 8, 512], F32, scope=esE)
            WEG = ctx.sbuf("l3f_weg", [128, 2, 8, 512], BF16, esE)
            WEU = ctx.sbuf("l3f_weu", [128, 2, 8, 512], BF16, esE)
            WED = ctx.sbuf("l3f_wed", [128, 2, 4, 1024], BF16, esE)
            CWR = ctx.sbuf("l3f_cwr", [1, 2, T], BF16, esE)
            CWB4 = ctx.sbuf("l3f_cwb4", [128, 4, 512], BF16, esE)
            egT = ctx.dram["L3f_egT"]
            euT = ctx.dram["L3f_euT"]
            edT = ctx.dram["L3f_edT"]
            ones1f = ctx.sb["c_ones1f"]
            sems = ctx.sem
            MM_H, EV_H, SC_H = 68, 68, 16
            B_mm, B_ev, B_sc = ctx.val["mm"], ctx.val["ev"], ctx.val["sc"]
            B_ldA, B_ldB = ctx.val["ldwA"], ctx.val["ldwB"]

            G.wait_ge(ctx.sem["st"], ctx.val["st"])
            for e in range(NE):
                sem = "ldwA" if e % 2 == 0 else "ldwB"
                if e >= 2:
                    G.wait_ge(ctx.sem["mm"], B_mm + (e - 1) * MM_H)
                ctx.inc(G.dma_start(WEG[:, e % 2, :, :], egT[e].rearrange("(c p) m -> p c m", p=128)), sem, 16)
                ctx.inc(G.dma_start(WEU[:, e % 2, :, :], euT[e].rearrange("(c p) m -> p c m", p=128)), sem, 16)
                ctx.inc(G.dma_start(WED[:, e % 2, :, :], edT[e].rearrange("(c p) m -> p c m", p=128)), sem, 16)
                ctx.inc(G.dma_start(CWR[0:1, e % 2, :], cwtd[e:e + 1, :]), sem, 16)

            class TR:
                def __init__(self, eng, reg, sem, base, per):
                    self.eng, self.reg, self.sem = eng, reg, sem
                    self.base, self.per, self.cur = base, per, 0

                def init(self, epv):
                    self.eng.reg_mov(self.reg, epv)
                    self.eng.reg_mul(self.reg, self.reg, self.per)
                    self.eng.reg_add(self.reg, self.reg, self.base)
                    self.cur = 0

                def wait(self, target):
                    if target > self.cur:
                        self.eng.reg_add(self.reg, self.reg, target - self.cur)
                        self.cur = target
                    self.eng.wait_ge(self.sem, self.reg)

            def pair_body(pz, epv, trs):
                t2b_prev = [0, 0, 0, 0]
                sil_prev = [0, 0, 0, 0]
                act4_prev = [0, 0, 0, 0]
                for half in range(2):
                    par = half
                    H_mm = half * MM_H
                    H_ev = half * EV_H
                    H_sc = half * SC_H
                    if pz == "PE":
                        PE.wait_ge(sems["ldwA" if half == 0 else "ldwB"],
                                   epv * 64 + ((B_ldA if half == 0 else B_ldB) + 64))
                        trs["pe_ev"].wait(H_ev)
                        for j in range(4):
                            PE.matmul(ep8[:, 4 + j, :], ctx.sb["c_ones1"][0:1, 0:128],
                                      CWR[0:1, par, 512 * j:512 * (j + 1)],
                                      start=True, stop=True).then_inc(sems["mm"], 1)
                    if pz == "V":
                        trs["v_mm"].wait(H_mm + 4)
                        trs["v_ev"].wait(H_ev)
                        for j in range(4):
                            V.tensor_copy(CWB4[:, j, :], ep8[:, 4 + j, :]).then_inc(sems["ev"], 1)
                    for n in range(4):
                        b_mm = H_mm + 4 + 16 * n
                        b_ev = H_ev + 4 + 16 * n
                        b_sc = H_sc + 4 * n
                        mm_gate = [b_mm + m + 1 for m in range(4)]
                        mm_up = [b_mm + 4 + m + 1 for m in range(4)]
                        mm_dA = b_mm + 12
                        mm_dB = b_mm + 16
                        t2b_cur = [b_ev + m + 1 for m in range(4)]
                        act4_cur = [b_ev + 4 + m + 1 for m in range(4)]
                        stt_dA = b_ev + 12
                        sil_cur = [b_sc + m + 1 for m in range(4)]
                        if pz == "PE":
                            for m in range(4):
                                for c in range(DC):
                                    i3 = PE.matmul(ep8[:, m, :], WEG[:, par, c, 128 * m:128 * (m + 1)],
                                                   XH[:, c, 512 * n:512 * (n + 1)],
                                                   start=(c == 0), stop=(c == DC - 1))
                                i3.then_inc(sems["mm"], 1)
                            for m in range(4):
                                trs["pe_sc"].wait(sil_cur[m])
                                for c in range(DC):
                                    i3 = PE.matmul(ep8[:, m, :], WEU[:, par, c, 128 * m:128 * (m + 1)],
                                                   XH[:, c, 512 * n:512 * (n + 1)],
                                                   start=(c == 0), stop=(c == DC - 1))
                                i3.then_inc(sems["mm"], 1)
                        if pz == "SC":
                            for m in range(4):
                                trs["s_mm"].wait(mm_gate[m])
                                trs["s_ev"].wait(act4_prev[m])
                                SC.activation(SIL[:, m, :], ep8[:, m, :], AF.Silu).then_inc(sems["sc"], 1)
                        if pz == "V":
                            for m in range(4):
                                trs["v_mm"].wait(mm_up[m])
                                if n == 0 and m == 0:
                                    trs["v_ev"].wait(H_ev + 4)
                                V.tensor_tensor(T2B[:, m, :], ep8[:, m, :], CWB4[:, n, :],
                                                ALU.mult).then_inc(sems["ev"], 1)
                            for m in range(4):
                                trs["v_sc"].wait(sil_cur[m])
                                trs["v_ev"].wait(t2b_cur[m])
                                V.tensor_tensor(ACT4[:, m, :], SIL[:, m, :], T2B[:, m, :],
                                                ALU.mult).then_inc(sems["ev"], 1)
                        if pz == "PE":
                            for hm in range(4):
                                for c in range(4):
                                    if hm == 0:
                                        trs["pe_ev"].wait(act4_cur[c])
                                    i3 = PE.matmul(ep8[:, 4 + hm, :], WED[:, par, c, 128 * hm:128 * (hm + 1)],
                                                   ACT4[:, c, :], start=(c == 0), stop=(c == 3))
                                i3.then_inc(sems["mm"], 1)
                        if pz == "V":
                            trs["v_ev"].wait(act4_cur[3])
                            for hm in range(4):
                                trs["v_mm"].wait(b_mm + 8 + hm + 1)
                                rsl = ROUT[:, hm, 512 * n:512 * (n + 1)]
                                V.scalar_tensor_tensor(rsl, ep8[:, 4 + hm, :], 0.5, rsl,
                                                       ALU.mult, ALU.add).then_inc(sems["ev"], 1)
                        if pz == "PE":
                            for hm in range(4, 8):
                                trs["pe_ev"].wait(b_ev + 8 + (hm - 4) + 1)
                                for c in range(4):
                                    i3 = PE.matmul(ep8[:, 4 + hm - 4, :], WED[:, par, c, 128 * hm:128 * (hm + 1)],
                                                   ACT4[:, c, :], start=(c == 0), stop=(c == 3))
                                i3.then_inc(sems["mm"], 1)
                        if pz == "V":
                            for hm in range(4, 8):
                                trs["v_mm"].wait(b_mm + 12 + (hm - 4) + 1)
                                rsl = ROUT[:, hm, 512 * n:512 * (n + 1)]
                                V.scalar_tensor_tensor(rsl, ep8[:, 4 + hm - 4, :], 0.5, rsl,
                                                       ALU.mult, ALU.add).then_inc(sems["ev"], 1)
                        t2b_prev = t2b_cur
                        sil_prev = sil_cur
                        act4_prev = act4_cur

            specs = {"pe_ev": (PE, "ev", 2 * EV_H), "pe_sc": (PE, "sc", 2 * SC_H),
                     "v_mm": (V, "mm", 2 * MM_H), "v_sc": (V, "sc", 2 * SC_H),
                     "v_ev": (V, "ev", 2 * EV_H),
                     "s_mm": (SC, "mm", 2 * MM_H), "s_ev": (SC, "ev", 2 * EV_H)}
            bases = {"mm": B_mm, "ev": B_ev, "sc": B_sc}
            ctx.uid += 1
            regs = {nm: esE.enter_context(e0.register("r%d_%s" % (ctx.uid, nm)))
                    for nm, (e0, _, _) in specs.items()}
            for pz, eng in (("PE", PE), ("V", V), ("SC", SC)):
                with eng.Fori(0, NE // 2) as epv:
                    trs = {}
                    for nm, (e0, sname, per) in specs.items():
                        if e0 is eng:
                            tr = TR(eng, regs[nm], ctx.sem[sname], bases[sname], per)
                            tr.init(epv)
                            trs[nm] = tr
                    pair_body(pz, epv, trs)
            ctx.val["mm"] = B_mm + (NE // 2) * 2 * MM_H
            ctx.val["ev"] = B_ev + (NE // 2) * 2 * EV_H
            ctx.val["sc"] = B_sc + (NE // 2) * 2 * SC_H
        ctx.barrier()
        ctx.inc(G.dma_start(X[:, :, :], xsp.rearrange("(c p) t -> p c t", p=128),
                            accum_op=ALU.add), "ldw", 16)
        G.wait_ge(ctx.sem["ldw"], ctx.val["ldw"])
        ctx.barrier()


def prep_l3_ffn(inputs, wmap):
    wmap["L3f_gwT"] = _t(inputs["L3_ffn_gate_w"])
    wmap["L3f_gbr"] = np.asarray(inputs["L3_ffn_gate_bias"], np.float32).reshape(1, -1)
    wmap["L3f_egT"] = _t(inputs["L3_ffn_eg_w"])
    wmap["L3f_euT"] = _t(inputs["L3_ffn_eu_w"])
    wmap["L3f_edT"] = _t(inputs["L3_ffn_ed_w"])
    wmap["L3f_sgT"] = _t(inputs["L3_ffn_sg_w"])
    wmap["L3f_suT"] = _t(inputs["L3_ffn_su_w"])
    wmap["L3f_sdT"] = _t(inputs["L3_ffn_sd_w"])


def prep_l2_attn(inputs, wmap):
    wq = np.asarray(inputs["L2_attn_W_qkvf"], np.float32)
    wmap["L2a_wqkvfT"] = _t(wq[0:1024])
    wmap["L2a_wkT"] = _t(wq[1024:2048])
    wmap["L2a_wvT"] = _t(wq[2048:3072])
    wmap["L2a_wfT"] = _t(wq[3072:4096])
    wmap["L2a_wconv"] = np.ascontiguousarray(
        np.transpose(np.asarray(inputs["L2_attn_W_conv"], np.float32), (2, 0, 1))
    ).reshape(4, 64, 64).transpose(1, 0, 2).copy()
    wmap["L2a_wprojT"] = _t(inputs["L2_attn_W_proj"])
    wmap["L2a_wff1"] = np.asarray(inputs["L2_attn_W_ff1"], np.float32)
    wmap["L2a_wff2"] = np.asarray(inputs["L2_attn_W_ff2"], np.float32)
    wmap["L2a_ln1g"] = np.ascontiguousarray(np.asarray(inputs["L2_attn_ln1_g"], np.float32).reshape(8, 128).T)
    wmap["L2a_ln1b"] = np.ascontiguousarray(np.asarray(inputs["L2_attn_ln1_b"], np.float32).reshape(8, 128).T)
    wmap["L2a_ln2g"] = np.ascontiguousarray(np.asarray(inputs["L2_attn_ln2_g"], np.float32).reshape(8, 128).T)
    wmap["L2a_ln2b"] = np.ascontiguousarray(np.asarray(inputs["L2_attn_ln2_b"], np.float32).reshape(8, 128).T)
    if "c_tril" not in wmap:
        wmap["c_tril"] = np.tril(np.ones((128, 128), np.float32)).astype(BF)
        hp = np.zeros((128, 2), np.float32)
        hp[0:64, 0] = 1.0
        hp[64:128, 1] = 1.0
        wmap["c_hpair"] = hp.astype(BF)


def prep_l2_ffn(inputs, wmap):
    wmap["L2f_rwT"] = _t(inputs["L2_ffn_router_weight"])
    wmap["L2f_gupT"] = _t(inputs["L2_ffn_gate_up_proj"])
    wmap["L2f_dpT"] = _t(inputs["L2_ffn_down_proj"])
    wmap["L2f_sgT"] = _t(inputs["L2_ffn_shared_gate_w"])
    wmap["L2f_suT"] = _t(inputs["L2_ffn_shared_up_w"])
    wmap["L2f_sdT"] = _t(inputs["L2_ffn_shared_down_w"])
    wmap["L2f_segT"] = _t(inputs["L2_ffn_shared_expert_gate_w"])


def prep_l1_attn(inputs, wmap):
    wmap["L1a_wq"] = np.asarray(inputs["L1_attn_Wq"], np.float32)
    wmap["L1a_wk"] = np.asarray(inputs["L1_attn_Wk"], np.float32)
    wmap["L1a_wv"] = np.asarray(inputs["L1_attn_Wv"], np.float32)
    wmap["L1a_wo"] = np.asarray(inputs["L1_attn_Wo"], np.float32)
    wmap["L1a_bqr"] = _row(inputs["L1_attn_bq"])
    wmap["L1a_bkr"] = _row(inputs["L1_attn_bk"])
    wmap["L1a_bvr"] = _row(inputs["L1_attn_bv"])
    wmap["L1a_bor"] = _row(inputs["L1_attn_bo"])
    wmap["L1a_nf"] = np.asarray(inputs["L1_attn_nrope_freq"], np.float32).reshape(1, 1)


def prep_l1_ffn(inputs, wmap):
    wmap["L1f_wr"] = np.asarray(inputs["L1_ffn_w_freq_real"], np.float32)
    wmap["L1f_wi"] = np.asarray(inputs["L1_ffn_w_freq_imag"], np.float32)
    wmap["L1f_br"] = _row(inputs["L1_ffn_bias"])
    if "c_dftChi" not in wmap:
        idx = np.arange(D, dtype=np.int64)
        ang = 2.0 * np.pi * ((idx[:, None] * idx[None, :]) % D).astype(np.float64) / D
        def hilo(a):
            a = a.astype(np.float32)
            hi = a.astype(BF)
            lo = (a - hi.astype(np.float32)).astype(BF)
            return hi, lo
        wmap["c_dftChi"], wmap["c_dftClo"] = hilo(np.cos(ang))
        wmap["c_dftSphi"], wmap["c_dftSplo"] = hilo(np.sin(ang))
        wmap["c_dftSnhi"], wmap["c_dftSnlo"] = hilo(-np.sin(ang))




def prep_l0_ffn(inputs, wmap):
    wmap["L0f_w1"] = np.asarray(inputs["L0_ffn_cn_w1"], np.float32)
    wmap["L0f_w2"] = np.asarray(inputs["L0_ffn_cn_w2"], np.float32)
    wmap["L0f_w3"] = np.asarray(inputs["L0_ffn_cn_w3"], np.float32)
    wmap["L0f_w4"] = np.asarray(inputs["L0_ffn_cn_w4"], np.float32)
    wmap["L0f_gw1"] = np.asarray(inputs["L0_ffn_gc_w1"], np.float32)
    wmap["L0f_gw2"] = np.asarray(inputs["L0_ffn_gc_w2"], np.float32)
    wmap["L0f_gw3"] = np.asarray(inputs["L0_ffn_gc_w3"], np.float32)
    wmap["L0f_b1r"] = _row(inputs["L0_ffn_cn_b1"])
    wmap["L0f_b2r"] = _row(inputs["L0_ffn_cn_b2"])
    wmap["L0f_b3r"] = _row(inputs["L0_ffn_cn_b3"])
    wmap["L0f_b4r"] = _row(inputs["L0_ffn_cn_b4"])
    wmap["L0f_gb1r"] = _row(inputs["L0_ffn_gc_b1"])
    wmap["L0f_gb2r"] = _row(inputs["L0_ffn_gc_b2"])
    wmap["L0f_gb3r"] = _row(inputs["L0_ffn_gc_b3"])
    wmap["L0f_ar"] = np.asarray(inputs["L0_ffn_adaptive_range"], np.float32).reshape(1, 1)



def host_consts():
    return {
        "c_ones128": np.full((128, 128), 1.0 / D, dtype=np.float32),
        "c_ones1": np.ones((1, 512), dtype=BF),
        "c_ones1f": np.ones((1, 128), dtype=np.float32),
        "c_ident": np.eye(128, dtype=BF),
        "c_identf": np.eye(128, dtype=np.float32),
        "c_j16": (np.arange(16, dtype=np.float32) / 16.0).reshape(1, 16),
        "c_posf": np.arange(T, dtype=np.float32).reshape(1, T),
        "c_onesq": np.ones((128, 64), dtype=BF),
        "c_avgc": _avgc(),
        "c_cmask": _cmask(),
    }


def _avgc():
    a = np.zeros((128, 256), dtype=np.float32)
    for tc in range(16):
        a[:, 16 * tc + tc] = 1.0 / 128
    return a.astype(BF)


def _cmask():
    m = np.zeros((128, 256), dtype=np.float32)
    for c in range(16):
        for j in range(16):
            if j > c:
                m[:, 16 * c + j] = -1e30
    return m



DT_MAP = {np.dtype(np.float32): F32, np.dtype(BF): BF16}


def build_program(emitters, wmap, consts):
    nc = bass.Bass("TRN2", target_bir_lowering=False)
    es = contextlib.ExitStack()
    ctx = Ctx(nc, es)
    xin = nc.dram_tensor("x_in", [D, T], F32, kind="ExternalInput").ap()
    xout = nc.dram_tensor("x_out", [D, T], F32, kind="ExternalOutput").ap()
    for name, arr in wmap.items():
        ctx.dram[name] = nc.dram_tensor(name, list(arr.shape), DT_MAP[arr.dtype],
                                        kind="ExternalInput").ap()
    for name, arr in consts.items():
        ctx.dram[name] = nc.dram_tensor(name, list(arr.shape), DT_MAP[arr.dtype],
                                        kind="ExternalInput").ap()
    with es:
        for s in ["ldx", "ldw", "ldwA", "ldwB", "mm", "ev", "sc", "pl", "st"]:
            ctx.new_sem(s)
        X = ctx.sbuf("X", [128, DC, T], F32)
        XH = ctx.sbuf("XH", [128, DC, T], BF16)
        ctx.sbuf("scr", [128, 2048], F32)
        GLOBAL_CONSTS = {"c_ones128", "c_ones1", "c_ones1f", "c_ident", "c_identf"}
        for cname, arr in consts.items():
            if cname not in GLOBAL_CONSTS:
                continue
            ctx.sbuf(cname, list(arr.shape), DT_MAP[arr.dtype])
            ctx.inc(nc.sync.dma_start(ctx.sb[cname][...], ctx.dram[cname]), "ldx", 16)
        ctx.inc(nc.sync.dma_start(X[:, :, :], xin.rearrange("(c p) t -> p c t", p=128)), "ldx", 16)
        nc.sync.wait_ge(ctx.sem["ldx"], ctx.val["ldx"])
        ctx.barrier()
        for emit in emitters:
            emit(ctx)
            ctx.barrier()
        ctx.inc(nc.sync.dma_start(xout.rearrange("(c p) t -> p c t", p=128), X[:, :, :]), "st", 16)
        nc.sync.wait_ge(ctx.sem["st"], ctx.val["st"])
    return nc


PREPS = {"l0_ffn": prep_l0_ffn, "l0_attn": prep_l0_attn,
         "l1_ffn": prep_l1_ffn, "l1_attn": prep_l1_attn,
         "l2_ffn": prep_l2_ffn, "l2_attn": prep_l2_attn,
         "l3_ffn": prep_l3_ffn, "l3_attn": prep_l3_attn}
EMITS = {"l0_ffn": emit_l0_ffn, "l0_attn": emit_l0_attn,
         "l1_ffn": emit_l1_ffn, "l1_attn": emit_l1_attn,
         "l2_ffn": emit_l2_ffn, "l2_attn": emit_l2_attn,
         "l3_ffn": emit_l3_ffn, "l3_attn": emit_l3_attn}


def build_chain(emit_names, inputs):
    wmap = {}
    done = set()
    emitters = []
    for n in emit_names:
        if n not in done:
            PREPS[n](inputs, wmap)
            done.add(n)
        emitters.append(EMITS[n])
    consts = host_consts()
    nc = build_program(emitters, wmap, consts)
    return nc, wmap, consts


def run_partial(x_full, inputs, emit_names, cores=None):
    nc, wmap, consts = build_chain(emit_names, inputs)
    cores = cores or list(range(NCORES))
    in_maps = []
    for ci in range(len(cores)):
        m = {"x_in": np.ascontiguousarray(x_full[ci].T)}
        m.update(wmap)
        m.update(consts)
        in_maps.append(m)
    res = run_bass_kernel_spmd(nc, in_maps, core_ids=cores)
    return np.stack([r["x_out"].T for r in res.results])


STEPS12 = ["l0_ffn", "l0_attn", "l0_ffn", "l1_ffn", "l1_attn", "l1_ffn",
           "l2_ffn", "l2_attn", "l2_ffn", "l3_ffn", "l3_attn", "l3_ffn"]

GROUPS = [tuple(STEPS12)]

_PROG = {}


def kernel(**inputs):
    x = np.asarray(inputs["L0_attn_hidden_states"], np.float32)
    xt = np.stack([np.ascontiguousarray(x[b].T) for b in range(NCORES)])
    consts = host_consts()
    for g in GROUPS:
        wmap = {}
        done = set()
        for n in g:
            if n not in done:
                PREPS[n](inputs, wmap)
                done.add(n)
        if g not in _PROG:
            _PROG[g] = build_program([EMITS[n] for n in g], wmap, consts)
        nc = _PROG[g]
        in_maps = []
        for ci in range(NCORES):
            m = {"x_in": xt[ci]}
            m.update(wmap)
            m.update(consts)
            in_maps.append(m)
        res = run_bass_kernel_spmd(nc, in_maps, core_ids=list(range(NCORES)))
        xt = np.stack([r["x_out"] for r in res.results])
    return np.ascontiguousarray(np.transpose(xt, (0, 2, 1)))
